# version 12
# baseline (speedup 1.0000x reference)
import numpy as np
import concourse.bass as bass
import concourse.mybir as mybir
from concourse.bass_utils import run_bass_kernel_spmd

F32 = mybir.dt.float32
BF16 = mybir.dt.bfloat16
I32 = mybir.dt.int32
AF = mybir.ActivationFunctionType
ALU = mybir.AluOpType
AX = mybir.AxisListType

D = 2048
SEQ = 4096
BATCH = 2
NCORE = 8
SLAB = SEQ // NCORE
NTOK = BATCH * SLAB
NTILE = NTOK // 128
IM = 64
HM = 16
HT = 16
EPS = 1e-5

OFF_IDENT = 0x04800
OFF_ONESC = 0x04900
OFF_ONES2 = 0x04920
OFF_SGS = 0x04A20
OFF_SUS = 0x04B20
OFF_SDS = 0x04C20
OFF_COLS = 0x04C60
OFF_VCOLS = 0x08660
OFF_SCOLS = 0x08760
OFF_LOTMP = 0x04E60
OFF_SCORR = 0x05660
OFF_SHCORR = 0x05E60
OFF_JUNK = 0x06660
OFF_HIF = 0x07660
OFF_XNT = 0x08800
OFF_GMIXT = 0x08800
OFF_HT = 0x10800
OFF_XT0 = 0x10800
OFF_XNN0 = 0x1C800
OFF_S = 0x10800
OFF_XCB = 0x20800
OFF_PAD = 0x2C800
OFF_WMB = 0x30C00
OFF_XT6 = 0x20800
OFF_OST = 0x24800
OFF_WGB = 0x31800
OFF_WUB = 0x33800
OFF_WDR = 0x31800
OFF_T1 = 0x35800
WDR_SLOTS = 6


class Pg:

    def __init__(self, nc):
        self.nc = nc
        self.sems = {}
        self.cnt = {}

    def sem(self, name):
        self.sems[name] = self.nc.alloc_semaphore(name)
        self.cnt[name] = 0

    def inc(self, ins, name, n=1):
        ins.then_inc(self.sems[name], n)
        self.cnt[name] += n
        return self.cnt[name]

    def dma(self, eng, name, out, in_):
        ins = eng.dma_start(out=out, in_=in_)
        return self.inc(ins, name, 16)

    def wait(self, eng, name, v):
        if v > 0:
            eng.wait_ge(self.sems[name], v)

    def c(self, name):
        return self.cnt[name]


def build_program(plans):
    nc = bass.Bass("TRN2", target_bir_lowering=False, enable_partition_id=True)

    xown = nc.dram_tensor("xown", [BATCH, SLAB, D], F32, kind="ExternalInput").ap()
    xfull = nc.dram_tensor("xfull", [BATCH, SEQ, D], F32, kind="ExternalInput").ap()
    wg = nc.dram_tensor("wg", [IM, 128, HT * 128], I32, kind="ExternalInput").ap()
    wu = nc.dram_tensor("wu", [IM, 128, HT * 128], I32, kind="ExternalInput").ap()
    wd = nc.dram_tensor("wd", [HM, 128, IM * 128], I32, kind="ExternalInput").ap()
    sg_in = nc.dram_tensor("sg_in", [128, IM], F32, kind="ExternalInput").ap()
    su_in = nc.dram_tensor("su_in", [128, IM], F32, kind="ExternalInput").ap()
    sd_in = nc.dram_tensor("sd_in", [128, HM], F32, kind="ExternalInput").ap()
    wmix = nc.dram_tensor("wmix", [128, 128, 512], F32, kind="ExternalInput").ap()
    y = nc.dram_tensor("y", [BATCH, SLAB, D], F32, kind="ExternalOutput").ap()

    at = nc.alloc_sbuf_tensor_at
    ident = at("ident", [128, 128], BF16, offset=OFF_IDENT)
    ones_col = at("ones_col", [128, 1], BF16, offset=OFF_ONESC)
    ones2 = at("ones2", [2, 128], BF16, offset=OFF_ONES2)
    sgs = at("sgs", [128, IM], F32, offset=OFF_SGS)
    sus = at("sus", [128, IM], F32, offset=OFF_SUS)
    sds = at("sds", [128, HM], F32, offset=OFF_SDS)
    colt = at("colt", [128, 16], F32, offset=OFF_COLS)
    lo_tmp = at("lo_tmp", [1, 1024], BF16, offset=OFF_LOTMP)
    scorr = at("scorr", [2, 1024], BF16, offset=OFF_SCORR)
    shcorr = at("shcorr", [2, 1024], BF16, offset=OFF_SHCORR)
    junk = at("junk", [128, 2048], BF16, offset=OFF_JUNK)
    hi_f = at("hi_f", [1, 1024], F32, offset=OFF_HIF)
    row_tmp = at("row_tmp", [1, 1024], F32, offset=OFF_JUNK)
    xnT = at("xnT", [128, HT, 1024], BF16, offset=OFF_XNT)
    gmixT = at("gmixT", [128, HM, 1024], BF16, offset=OFF_GMIXT)
    hT = at("hT", [128, IM, 1024], BF16, offset=OFF_HT)
    xt = [at(f"xt{i}", [128, 2048], F32, offset=OFF_XT0 + i * 0x2000) for i in range(6)]
    xnn_f = [at(f"xnnf{i}", [128, 2048], F32, offset=OFF_XNN0 + i * 0x2000) for i in range(2)]
    xnn = [at(f"xnn{i}", [128, 2048], BF16, offset=OFF_XNN0 + 0x4000 + i * 0x1000) for i in range(2)]
    padf = [at(f"padf{i}", [128, 2048], F32, offset=0x32800 + i * 0x2000) for i in range(2)]
    s_acc = at("s_acc", [128, NTILE, 2048], F32, offset=OFF_S)
    xcb = [at(f"xcb{i}", [128, 2048], F32, offset=OFF_XCB + i * 0x2000) for i in range(6)]
    pad = [at(f"pad{i}", [128, 2080], BF16, offset=OFF_PAD + i * 0x1080) for i in range(4)]
    wmb = [at(f"wmb{i}", [128, 512], BF16, offset=OFF_WMB + i * 0x400) for i in range(6)]
    vcols = at("vcols", [128, 64], F32, offset=OFF_VCOLS)
    scols = at("scols", [128, 32], F32, offset=OFF_SCOLS)
    xt6 = [at(f"xt6{i}", [128, 2048], F32, offset=OFF_XT6 + i * 0x2000) for i in range(2)]
    ost = [at(f"ost{i}", [128, 2048], F32, offset=OFF_OST + i * 0x2000) for i in range(2)]
    wgb = [at(f"wgb{i}", [128, 2048], BF16, offset=OFF_WGB + i * 0x1000) for i in range(2)]
    wub = [at(f"wub{i}", [128, 2048], BF16, offset=OFF_WUB + i * 0x1000) for i in range(2)]
    wdr = [at(f"wdr{i}", [128, 2048], BF16, offset=OFF_WDR + i * 0x1000) for i in range(WDR_SLOTS)]
    t1 = [at(f"t1_{i}", [128, 1024], F32, offset=OFF_T1 + i * 0x1000) for i in range(2)]

    p = Pg(nc)
    sems = ["qSc", "qXT0", "qXT1", "qCorr", "qS0", "qS1", "gEm",
            "qWG0", "qWG1", "qWU0", "qWU1",
            "tMM", "vE", "sE", "gE",
            "tMMm", "vEm", "sEm", "mph",
            "qXC0", "qXC1", "qWM0", "qWM1"]
    sems += [f"qWD{i}" for i in range(WDR_SLOTS)]
    sems += [f"qXTs{i}" for i in range(6)]
    sems += [f"qXCs{i}" for i in range(6)]
    sems += [f"qWMs{i}" for i in range(6)]
    for name in sems:
        p.sem(name)

    V, S, T, G, SY = nc.vector, nc.scalar, nc.tensor, nc.gpsimd, nc.sync

    for _h in p.sems.values():
        nc.gpsimd.sem_clear(_h)
    nc.all_engine_barrier()

    def col(k):
        return colt[:, k:k + 1]

    C_SUM, C_M, C_VT, C_V1 = 0, 1, 2, 3
    C_SQ, C_SD, C_SFEN = 4, 5, 6
    C_VE, C_RS, C_NB, C_VFEN = 8, 9, 10, 11

    import ml_dtypes as _mld
    ident_dram = nc.inline_tensor(
        np.eye(128, dtype=_mld.bfloat16), name="ident_const").ap()
    ident_gE = p.dma(SY, "qSc", ident[:], ident_dram)

    p.inc(V.memset(ones_col[:], 1.0), "vE")
    p.inc(V.memset(ones2[:], 1.0), "vE")
    ones_vE = p.c("vE")

    p.dma(SY, "qSc", sgs[:], sg_in)
    p.dma(SY, "qSc", sus[:], su_in)
    p.dma(SY, "qSc", sds[:], sd_in)
    scales_qX = p.c("qSc")

    def xown_tile(i):
        n, s0 = i // (SLAB // 128), (i * 128) % SLAB
        return xown[n, s0:s0 + 128, :]

    nvops = [0]

    def vdummy():
        V.tensor_copy(vcols[:, 62:63], vcols[:, 63:64])
        nvops[0] += 1

    def vpad(target):
        while nvops[0] < target:
            vdummy()

    def vop(ins):
        nvops[0] += 1
        return ins

    def vc(i, k):
        return vcols[:, 6 * i + k:6 * i + k + 1]

    def scc(i, k):
        return scols[:, 2 * i + k:2 * i + k + 1]

    V_SUM, V_NM, V_VT, V_V1, V_VE, V_RS = range(6)
    S_SQ, S_SD = 0, 1

    with nc.psum_tensor("ptr0", [128, 1024], BF16) as ptr0, \
         nc.psum_tensor("ptr1", [128, 1024], BF16) as ptr1, \
         nc.psum_tensor("psS", [1, 1024], F32) as psS:
        ptr = [ptr0, ptr1]
        tr_tMM = [0] * NTILE
        xnt_sE = [0] * NTILE
        xnt_half_sE = [[0, 0] for _ in range(NTILE)]
        xtile_q = [0] * NTILE
        sq2_sE = [0] * NTILE
        sd_sE = [0] * NTILE
        rs_vop = [0] * NTILE
        apply_vE = [0] * NTILE
        V.memset(vcols[:, 62:63], 0.0)
        V.memset(vcols[:, 63:64], 0.0)
        for step in range(NTILE + 4):
            if step < NTILE:
                i = step
                sl = i % 6
                qn = f"qXTs{sl}"
                if i >= 6:
                    p.wait(SY, "vE", apply_vE[i - 6])
                    p.wait(SY, "sE", sq2_sE[i - 6])
                xtile_q[i] = p.dma(SY, qn, xt[sl][:], xown_tile(i))
                p.wait(V, qn, xtile_q[i])
                vop(V.reduce_sum(vc(i, V_SUM), xt[sl][:], axis=AX.X))
                p.wait(S, qn, xtile_q[i])
                ins = S.activation(junk[:], xt[sl][:], AF.Square,
                                   accum_out=scc(i, S_SQ))
                sq2_sE[i] = p.inc(ins, "sE")
            if step >= 2 and step - 2 < NTILE:
                j = step - 2
                vop(V.tensor_scalar_mul(vc(j, V_NM), vc(j, V_SUM), -1.0 / D))
                p.wait(V, "sE", sq2_sE[j])
                vop(V.tensor_scalar(vc(j, V_V1), scc(j, S_SQ), 1.0 / D, EPS,
                                    ALU.mult, ALU.add))
                vop(V.tensor_mul(vc(j, V_VT), vc(j, V_NM), vc(j, V_NM)))
                vdummy()
                vdummy()
                ins = vop(V.tensor_sub(vc(j, V_VE), vc(j, V_V1), vc(j, V_VT)))
                ve_vE = p.inc(ins, "vE")
                p.wait(S, "vE", ve_vE)
                ins = S.activation(scc(j, S_SD), vc(j, V_VE), AF.Sqrt)
                sd_sE[j] = p.inc(ins, "sE")
                p.wait(V, "sE", sd_sE[j])
                vop(V.reciprocal(vc(j, V_RS), scc(j, S_SD)))
                rs_vop[j] = nvops[0]
            if step >= 4:
                k = step - 4
                sl = k % 6
                parn = k % 2
                vpad(rs_vop[k] + 8)
                if k >= 2:
                    p.wait(V, "tMM", tr_tMM[k - 2])
                ins = vop(V.tensor_scalar(xnn_f[parn][:], xt[sl][:],
                                          vc(k, V_NM), vc(k, V_RS),
                                          ALU.add, ALU.mult))
                apply_vE[k] = p.inc(ins, "vE")
                p.wait(S, "vE", apply_vE[k])
                ins = S.copy(xnn[parn][:], xnn_f[parn][:])
                cast_sE = p.inc(ins, "sE")

                p.wait(T, "qSc", ident_gE)
                p.wait(T, "sE", cast_sE)
                for half in range(2):
                    if k >= 1:
                        p.wait(T, "sE", xnt_half_sE[k - 1][half])
                    for jj in range(8):
                        h = half * 8 + jj
                        ins = T.transpose(ptr[half][:, jj * 128:(jj + 1) * 128],
                                          xnn[parn][:, h * 128:(h + 1) * 128],
                                          ident[:])
                    tr_half = p.inc(ins, "tMM")
                    if half == 1:
                        tr_tMM[k] = tr_half
                    p.wait(S, "tMM", tr_half)
                    ins = S.copy(xnT[:, half * 8:(half + 1) * 8,
                                     k * 128:(k + 1) * 128],
                                 ptr[half][:].rearrange("p (a b) -> p a b", a=8))
                    xnt_half_sE[k][half] = p.inc(ins, "sE")
                    xnt_sE[k] = xnt_half_sE[k][half]
        all_xnT_sE = p.c("sE")

        p.wait(T, "sE", all_xnT_sE)
        p.wait(T, "vE", ones_vE)
        for c in range(2):
            for h in range(HT):
                ins = T.matmul(psS[:, c * 512:(c + 1) * 512], ones_col[:],
                               xnT[:, h, c * 512:(c + 1) * 512],
                               start=(h == 0), stop=(h == HT - 1))
            p.inc(ins, "tMM")
        srow_tMM = p.c("tMM")
        p.wait(S, "tMM", srow_tMM)
        ins = S.activation(scorr[0:1, :], psS[:], AF.Copy, scale=-128.0)
        hi_sE = p.inc(ins, "sE")
        p.wait(V, "sE", hi_sE)
        V.tensor_scalar_mul(row_tmp[:], psS[:], -128.0)
        V.tensor_copy(hi_f[:], scorr[0:1, :])
        vdummy()
        vdummy()
        ins = V.tensor_sub(lo_tmp[:], row_tmp[:], hi_f[:])
        lo_vE = p.inc(ins, "vE")
        p.wait(SY, "vE", lo_vE)
        scorr_qX = p.dma(SY, "qCorr", scorr[1:2, :], lo_tmp[:])
        scorr_sE = hi_sE

    with nc.psum_tensor("pg0", [128, 512]) as pg0, \
         nc.psum_tensor("pg1", [128, 512]) as pg1, \
         nc.psum_tensor("pg2", [128, 512]) as pg2, \
         nc.psum_tensor("pg3", [128, 512]) as pg3, \
         nc.psum_tensor("pg4", [128, 512]) as pg4, \
         nc.psum_tensor("pg5", [128, 512]) as pg5, \
         nc.psum_tensor("psSh", [1, 1024], F32) as psSh:
        pg = [pg0, pg1, pg2, pg3, pg4, pg5]
        wdma_qW = [0] * IM
        chain_done_tMM = [0] * IM
        evac_sE = {}
        evac_vE = {}
        stt_vE = [0] * IM
        silu_sE = [0] * IM

        def emit_wdma(m):
            if m >= 2:
                p.wait(G, "tMM", chain_done_tMM[m - 2])
            gq, uq = f"qWG{m % 2}", f"qWU{m % 2}"
            p.dma(G, gq, wgb[m % 2][:], wg[m])
            p.dma(G, uq, wub[m % 2][:], wu[m])
            wdma_qW[m] = (gq, p.c(gq), uq, p.c(uq))

        emit_wdma(0)
        emit_wdma(1)

        first_mm = True
        for m in range(IM):
            par = m % 2
            ch0 = 4 * m
            if first_mm:
                p.wait(T, "sE", scorr_sE)
                p.wait(T, "qCorr", scorr_qX)
                p.wait(T, "vE", ones_vE)
                first_mm = False
            gq, gv, uq, uv = wdma_qW[m]
            p.wait(T, gq, gv)
            p.wait(T, uq, uv)
            for j in range(4):
                chp = ch0 + j - 6
                if chp >= 0:
                    if chp in evac_sE:
                        p.wait(T, "sE", evac_sE.pop(chp))
                    else:
                        p.wait(T, "vE", evac_vE.pop(chp))
            bg = [pg[(ch0 + 0) % 6], pg[(ch0 + 1) % 6]]
            bu = [pg[(ch0 + 2) % 6], pg[(ch0 + 3) % 6]]
            for h in range(HT):
                for c in range(2):
                    T.matmul(bg[c][:], wgb[par][:, h * 128:(h + 1) * 128],
                             xnT[:, h, c * 512:(c + 1) * 512],
                             start=(h == 0), stop=False)
            g_tMM = [0, 0]
            for c in range(2):
                ins = T.matmul(bg[c][:], ones2[:], scorr[:, c * 512:(c + 1) * 512],
                               start=False, stop=True)
                g_tMM[c] = p.inc(ins, "tMM")
            for h in range(HT):
                for c in range(2):
                    T.matmul(bu[c][:], wub[par][:, h * 128:(h + 1) * 128],
                             xnT[:, h, c * 512:(c + 1) * 512],
                             start=(h == 0), stop=False)
            u_tMM = [0, 0]
            for c in range(2):
                ins = T.matmul(bu[c][:], ones2[:], scorr[:, c * 512:(c + 1) * 512],
                               start=False, stop=True)
                u_tMM[c] = p.inc(ins, "tMM")
            chain_done_tMM[m] = p.c("tMM")
            if m + 2 < IM:
                emit_wdma(m + 2)

            if m == 0:
                p.wait(S, "qSc", scales_qX)
                p.wait(V, "qSc", scales_qX)
            if m >= 2:
                p.wait(S, "vE", stt_vE[m - 2])
            sg_col = sgs[:, m:m + 1]
            s_cnt = [0, 0]
            for c in range(2):
                p.wait(S, "tMM", g_tMM[c])
                ins = S.activation(t1[par][:, c * 512:(c + 1) * 512], bg[c][:],
                                   AF.Silu, scale=sg_col)
                s_cnt[c] = p.inc(ins, "sE")
            evac_sE[ch0 + 0] = s_cnt[0]
            evac_sE[ch0 + 1] = s_cnt[1]
            silu_sE[m] = s_cnt[1]
            su_col = sus[:, m:m + 1]
            v_cnt = [0, 0]
            for c in range(2):
                p.wait(V, "tMM", u_tMM[c])
                p.wait(V, "sE", s_cnt[c])
                ins = V.scalar_tensor_tensor(hT[:, m, c * 512:(c + 1) * 512],
                                             bu[c][:], su_col,
                                             t1[par][:, c * 512:(c + 1) * 512],
                                             ALU.mult, ALU.mult)
                v_cnt[c] = p.inc(ins, "vE")
            evac_vE[ch0 + 2] = v_cnt[0]
            evac_vE[ch0 + 3] = v_cnt[1]
            stt_vE[m] = v_cnt[1]

            if m >= 2:
                ms = m - 2
                p.wait(T, "vE", stt_vE[ms])
                for c in range(2):
                    ins = T.matmul(psSh[:, c * 512:(c + 1) * 512], ones_col[:],
                                   hT[:, ms, c * 512:(c + 1) * 512],
                                   start=(ms == 0), stop=False)
        for ms in (IM - 2, IM - 1):
            p.wait(T, "vE", stt_vE[ms])
            for c in range(2):
                ins = T.matmul(psSh[:, c * 512:(c + 1) * 512], ones_col[:],
                               hT[:, ms, c * 512:(c + 1) * 512],
                               start=False, stop=(ms == IM - 1))
        sh_tMM = p.inc(ins, "tMM")
        p.wait(S, "tMM", sh_tMM)
        ins = S.activation(shcorr[0:1, :], psSh[:], AF.Copy, scale=-128.0)
        shhi_sE = p.inc(ins, "sE")
        p.wait(V, "sE", shhi_sE)
        V.tensor_scalar_mul(row_tmp[:], psSh[:], -128.0)
        V.tensor_copy(hi_f[:], shcorr[0:1, :])
        vdummy()
        vdummy()
        ins = V.tensor_sub(lo_tmp[:], row_tmp[:], hi_f[:])
        shlo_vE = p.inc(ins, "vE")
        p.wait(SY, "vE", shlo_vE)
        shcorr_qX = p.dma(SY, "qCorr", shcorr[1:2, :], lo_tmp[:])
        all_stt_vE = p.c("vE")
        all_silu_sE = max(silu_sE)

    with nc.psum_tensor("pd0", [128, 512]) as pd0, \
         nc.psum_tensor("pd1", [128, 512]) as pd1, \
         nc.psum_tensor("pd2", [128, 512]) as pd2, \
         nc.psum_tensor("pd3", [128, 512]) as pd3:
        pd = [pd0, pd1, pd2, pd3]
        sub_qW = {}
        sub_tMM = {}
        gm_sE = {}
        kseq = 0
        def emit_wd_dma(m, q):
            k = 4 * m + q
            if k >= WDR_SLOTS:
                p.wait(G, "tMM", sub_tMM[k - WDR_SLOTS])
            qd = f"qWD{k % WDR_SLOTS}"
            v = p.dma(G, qd, wdr[k % WDR_SLOTS][:],
                      wd[m, :, q * 2048:(q + 1) * 2048])
            sub_qW[(m, q)] = (qd, v)

        p.wait(G, "tMM", chain_done_tMM[IM - 1])
        p.wait(G, "vE", all_stt_vE)
        p.wait(G, "sE", all_silu_sE)
        for k0 in range(WDR_SLOTS):
            emit_wd_dma(k0 // 4, k0 % 4)

        first_mm = True
        for m in range(HM):
            ch0 = 2 * m
            if first_mm:
                p.wait(T, "vE", all_stt_vE)
                p.wait(T, "sE", all_silu_sE)
                p.wait(T, "tMM", sh_tMM)
                p.wait(T, "sE", shhi_sE)
                p.wait(T, "qCorr", shcorr_qX)
                first_mm = False
            for j in range(2):
                chp = ch0 + j - 4
                if chp >= 0:
                    p.wait(T, "sE", gm_sE.pop(chp))
            b0, b1 = pd[(ch0 + 0) % 4], pd[(ch0 + 1) % 4]
            for u in range(IM):
                q = u // 16
                if u % 16 == 0:
                    qd, vd = sub_qW[(m, q)]
                    p.wait(T, qd, vd)
                slot = (4 * m + q) % WDR_SLOTS
                lh = wdr[slot][:, (u % 16) * 128:(u % 16 + 1) * 128]
                T.matmul(b0[:], lh, hT[:, u, 0:512], start=(u == 0), stop=False)
                ins = T.matmul(b1[:], lh, hT[:, u, 512:1024], start=(u == 0), stop=False)
                if u % 16 == 15:
                    k = 4 * m + q
                    sub_tMM[k] = p.inc(ins, "tMM")
                    nk = k + WDR_SLOTS
                    if nk < 4 * HM:
                        emit_wd_dma(nk // 4, nk % 4)
            d_tMM = [0, 0]
            for c, b in ((0, b0), (1, b1)):
                ins = T.matmul(b[:], ones2[:], shcorr[:, c * 512:(c + 1) * 512],
                               start=False, stop=True)
                d_tMM[c] = p.inc(ins, "tMM")
            sd_col = sds[:, m:m + 1]
            for c, b in ((0, b0), (1, b1)):
                if m == 0 and c == 0:
                    p.wait(S, "qSc", scales_qX)
                p.wait(S, "tMM", d_tMM[c])
                ins = S.activation(gmixT[:, m, c * 512:(c + 1) * 512], b[:],
                                   AF.Silu, scale=sd_col)
                gm_sE[ch0 + c] = p.inc(ins, "sE")
        gmix_done_sE = p.c("sE")
        endP4_tMM = p.c("tMM")

    with nc.psum_tensor("pm0", [64, 512]) as pm0, \
         nc.psum_tensor("pm1", [64, 512]) as pm1, \
         nc.psum_tensor("pm2", [64, 512]) as pm2, \
         nc.psum_tensor("pm3", [64, 512]) as pm3, \
         nc.psum_tensor("pm4", [64, 512]) as pm4, \
         nc.psum_tensor("pm5", [64, 512]) as pm5, \
         nc.psum_tensor("pm6", [64, 512]) as pm6, \
         nc.psum_tensor("pm7", [64, 512]) as pm7:
        pm = [pm0, pm1, pm2, pm3, pm4, pm5, pm6, pm7]
        engines = [nc.tensor, nc.vector, nc.scalar, nc.gpsimd, nc.sync]
        idx = [e.partition_id() for e in engines]
        for core in nc.Switch(engines=engines, index=idx, n=NCORE):
            groups = plans[core]
            for nm in (["tMMm", "vEm", "sEm", "gEm"]
                       + [f"qXCs{sl}" for sl in range(6)]
                       + [f"qWMs{sl}" for sl in range(6)]):
                p.cnt[nm] = 0

            p.wait(V, "tMM", endP4_tMM)
            ins = V.memset(s_acc[:], 0.0)
            memset_vEm = p.inc(ins, "vEm")
            p.wait(G, "tMM", endP4_tMM)
            for _ps in range(4):
                ins = G.memset(pad[_ps][:], 0.0)
            padz_gEm = p.inc(ins, "gEm")

            insts = []
            for gidx, (r_loc, n, bs) in enumerate(groups):
                for ci, b in enumerate(bs):
                    insts.append((b, n, gidx, r_loc, ci == 0, ci == len(bs) - 1))
            K = len(insts)
            wq_l = [None] * K
            padf_vEm = [0] * K
            padcp = [0] * K
            sq2m = [0] * K
            rs_vopm = [0] * K
            applym = [0] * K
            mm_done = [0] * K
            grp_evac = {}
            first_T = True
            for t in range(K + 4):
                if t < K:
                    b, n, gidx, r_loc, gfirst, glast = insts[t]
                    sl = t % 6
                    colv = int(plans["cols"][b])
                    qc = f"qXCs{sl}"
                    if t == 0:
                        p.wait(SY, "tMM", endP4_tMM)
                    if t >= 6:
                        p.wait(SY, "vEm", padf_vEm[t - 6])
                        p.wait(SY, "sEm", sq2m[t - 6])
                    src = xfull[n, 64 * colv:64 * colv + 64, :]
                    p.dma(SY, qc, xcb[sl][0:64, :], src)
                    ld_q = p.dma(SY, qc, xcb[sl][64:128, :], src)
                    qm = f"qWMs{sl}"
                    if t == 0:
                        p.wait(G, "tMM", endP4_tMM)
                    if t >= 6:
                        p.wait(G, "tMMm", mm_done[t - 6])
                    wq_l[t] = (qm, p.dma(G, qm, wmb[sl][:], wmix[b]))
                    cs = t % 4
                    p.wait(V, qc, ld_q)
                    vop(V.reduce_sum(vc(cs, V_SUM), xcb[sl][:], axis=AX.X))
                    p.wait(S, qc, ld_q)
                    ins = S.activation(junk[:], xcb[sl][:], AF.Square,
                                       accum_out=scc(cs, S_SQ))
                    sq2m[t] = p.inc(ins, "sEm")
                if t >= 2 and t - 2 < K:
                    j = t - 2
                    cs = j % 4
                    vop(V.tensor_scalar_mul(vc(cs, V_NM), vc(cs, V_SUM), -1.0 / D))
                    p.wait(V, "sEm", sq2m[j])
                    vop(V.tensor_scalar(vc(cs, V_V1), scc(cs, S_SQ), 1.0 / D, EPS,
                                        ALU.mult, ALU.add))
                    vop(V.tensor_mul(vc(cs, V_VT), vc(cs, V_NM), vc(cs, V_NM)))
                    vdummy()
                    vdummy()
                    ins = vop(V.tensor_sub(vc(cs, V_VE), vc(cs, V_V1), vc(cs, V_VT)))
                    ve_v = p.inc(ins, "vEm")
                    p.wait(S, "vEm", ve_v)
                    ins = S.activation(scc(cs, S_SD), vc(cs, V_VE), AF.Sqrt)
                    sd_s = p.inc(ins, "sEm")
                    p.wait(V, "sEm", sd_s)
                    vop(V.reciprocal(vc(cs, V_RS), scc(cs, S_SD)))
                    rs_vopm[j] = nvops[0]
                if t >= 4:
                    k = t - 4
                    b, n, gidx, r_loc, gfirst, glast = insts[k]
                    sl = k % 6
                    ps = k % 4
                    cs = k % 4
                    pf = k % 2
                    vpad(rs_vopm[k] + 8)
                    if k >= 2:
                        p.wait(V, "sEm", padcp[k - 2])
                    ins = vop(V.tensor_scalar(padf[pf][:], xcb[sl][:],
                                              vc(cs, V_NM), vc(cs, V_RS),
                                              ALU.add, ALU.mult))
                    padf_vEm[k] = p.inc(ins, "vEm")
                    p.wait(S, "vEm", padf_vEm[k])
                    if k < 4:
                        p.wait(S, "gEm", padz_gEm)
                    else:
                        p.wait(S, "tMMm", mm_done[k - 4])
                    S.copy(pad[ps][0:64, 15:2063], padf[pf][0:64, :])
                    ins = S.copy(pad[ps][64:128, 14:2062], padf[pf][64:128, :])
                    padcp[k] = p.inc(ins, "sEm")
                    applym[k] = padcp[k]
                    banks = pm[4 * (gidx % 2):4 * (gidx % 2) + 4]
                    if first_T:
                        p.wait(T, "tMM", endP4_tMM)
                        p.wait(T, "sE", gmix_done_sE)
                        first_T = False
                    if gfirst and gidx >= 2:
                        p.wait(T, "sEm", grp_evac[gidx - 2])
                    p.wait(T, wq_l[k][0], wq_l[k][1])
                    p.wait(T, "sEm", applym[k])
                    ins = None
                    for jj in range(8):
                        lh = wmb[sl][:, jj * 64:(jj + 1) * 64]
                        for c in range(4):
                            ins = T.matmul(
                                banks[c][:], lh,
                                pad[ps][:, c * 512 + 2 * jj:c * 512 + 2 * jj + 512],
                                start=(gfirst and jj == 0),
                                stop=(glast and jj == 7))
                    mm_done[k] = p.inc(ins, "tMMm")
                    if glast:
                        tile_idx = n * (NTILE // 2) + r_loc // 2
                        p0 = (r_loc % 2) * 64
                        p.wait(S, "tMMm", mm_done[k])
                        p.wait(S, "vEm", memset_vEm)
                        ins = None
                        for c in range(4):
                            ins = S.copy(s_acc[p0:p0 + 64, tile_idx,
                                               c * 512:(c + 1) * 512],
                                         banks[c][0:64, :])
                        grp_evac[gidx] = p.inc(ins, "sEm")

            V.tensor_copy(col(C_VFEN), col(C_SUM)).then_inc(p.sems["mph"], 1)
            S.copy(col(C_SFEN), col(C_SQ)).then_inc(p.sems["mph"], 1)
            T.nop().then_inc(p.sems["mph"], 1)
            G.nop().then_inc(p.sems["mph"], 1)
            SY.nop().then_inc(p.sems["mph"], 1)
        p.cnt["mph"] = 5

    with nc.psum_tensor("pt60", [128, 2048], BF16) as pt60, \
         nc.psum_tensor("pt61", [128, 2048], BF16) as pt61:
        pt6 = [pt60, pt61]
        comb_vE = [0] * NTILE
        store_qS = [0] * NTILE
        tr6_tMM = [0] * NTILE
        for i in range(NTILE):
            par = i % 2
            if i == 0:
                p.wait(SY, "mph", 5)
                p.wait(T, "mph", 5)
                p.wait(V, "mph", 5)
                p.wait(S, "mph", 5)
            qn = "qXT0" if par == 0 else "qXT1"
            if i >= 2:
                p.wait(SY, "vE", comb_vE[i - 2])
            x_q = (qn, p.dma(SY, qn, xt6[par][:], xown_tile(i)))
            if i >= 2:
                p.wait(T, "vE", comb_vE[i - 2])
            ins = None
            for m in range(HM):
                ins = T.transpose(pt6[par][:, m * 128:(m + 1) * 128],
                                  gmixT[:, m, i * 128:(i + 1) * 128], ident[:])
            tr6_tMM[i] = p.inc(ins, "tMM")
            p.wait(V, x_q[0], x_q[1])
            p.wait(V, "tMM", tr6_tMM[i])
            if i >= 2:
                p.wait(V, store_qS[i - 2][0], store_qS[i - 2][1])
            V.tensor_tensor(ost[par][:], s_acc[:, i, :], pt6[par][:], ALU.mult)
            vdummy()
            vdummy()
            ins = V.tensor_add(ost[par][:], ost[par][:], xt6[par][:])
            comb_vE[i] = p.inc(ins, "vE")
            p.wait(S, "vE", comb_vE[i])
            n, s0 = i // (SLAB // 128), (i * 128) % SLAB
            qs = "qS0" if par == 0 else "qS1"
            store_qS[i] = (qs, p.dma(S, qs, y[n, s0:s0 + 128, :], ost[par][:]))
        p.wait(S, "qS0", p.c("qS0"))
        p.wait(S, "qS1", p.c("qS1"))
        p.wait(SY, "qS0", p.c("qS0"))
        p.wait(SY, "qS1", p.c("qS1"))

    return nc


def _plan(rows):
    plans = {}
    for c in range(NCORE):
        groups = []
        for r_loc in range(8):
            bs = [b for b in range(128) if int(rows[b]) == 8 * c + r_loc]
            if bs:
                for n in range(BATCH):
                    groups.append((r_loc, n, bs))
        plans[c] = groups
    return plans


def kernel(L0_attn_x, L0_attn_block_values, L0_attn_cols, L0_attn_rows,
           L0_ffn_gate_w, L0_ffn_gate_scale, L0_ffn_up_w, L0_ffn_up_scale,
           L0_ffn_down_w, L0_ffn_down_scale):
    x = np.ascontiguousarray(np.asarray(L0_attn_x, dtype=np.float32))
    blockv = np.asarray(L0_attn_block_values, dtype=np.float32)
    cols = np.asarray(L0_attn_cols, dtype=np.int32)
    rows = np.asarray(L0_attn_rows, dtype=np.int32)
    gw = np.asarray(L0_ffn_gate_w, dtype=np.int32)
    uw = np.asarray(L0_ffn_up_w, dtype=np.int32)
    dw = np.asarray(L0_ffn_down_w, dtype=np.int32)

    gp = np.ascontiguousarray(
        gw.reshape(IM, 128, HT, 128).transpose(0, 3, 2, 1)).reshape(IM, 128, HT * 128)
    up = np.ascontiguousarray(
        uw.reshape(IM, 128, HT, 128).transpose(0, 3, 2, 1)).reshape(IM, 128, HT * 128)
    dp = np.ascontiguousarray(
        dw.reshape(HM, 128, IM, 128).transpose(0, 3, 2, 1)).reshape(HM, 128, IM * 128)
    sgp = np.ascontiguousarray(
        np.asarray(L0_ffn_gate_scale, dtype=np.float32).reshape(IM, 128).T)
    sup = np.ascontiguousarray(
        np.asarray(L0_ffn_up_scale, dtype=np.float32).reshape(IM, 128).T)
    sdp = np.ascontiguousarray(
        np.asarray(L0_ffn_down_scale, dtype=np.float32).reshape(HM, 128).T)
    wm = np.ascontiguousarray(
        blockv.reshape(128, 64, 64, 8, 2).transpose(0, 4, 2, 3, 1)
    ).reshape(128, 128, 512)

    plans = _plan(rows)
    plans["cols"] = cols
    nc = build_program(plans)

    in_maps = []
    for c in range(NCORE):
        in_maps.append({
            "xown": np.ascontiguousarray(x[:, SLAB * c:SLAB * (c + 1), :]),
            "xfull": x,
            "wg": gp, "wu": up, "wd": dp,
            "sg_in": sgp, "su_in": sup, "sd_in": sdp,
            "wmix": wm,
        })
    res = run_bass_kernel_spmd(nc, in_maps, core_ids=list(range(NCORE)))
    out = np.empty((BATCH, SEQ, D), dtype=np.float32)
    for c in range(NCORE):
        out[:, SLAB * c:SLAB * (c + 1), :] = res.results[c]["y"]
    return out


# version 13
# speedup vs baseline: 1.0005x; 1.0005x over previous
import numpy as np
import concourse.bass as bass
import concourse.mybir as mybir
from concourse.bass_utils import run_bass_kernel_spmd

F32 = mybir.dt.float32
BF16 = mybir.dt.bfloat16
I32 = mybir.dt.int32
AF = mybir.ActivationFunctionType
ALU = mybir.AluOpType
AX = mybir.AxisListType

D = 2048
SEQ = 4096
BATCH = 2
NCORE = 8
SLAB = SEQ // NCORE
NTOK = BATCH * SLAB
NTILE = NTOK // 128
IM = 64
HM = 16
HT = 16
EPS = 1e-5

OFF_IDENT = 0x04800
OFF_ONESC = 0x04900
OFF_ONES2 = 0x04920
OFF_SGS = 0x04A20
OFF_SUS = 0x04B20
OFF_SDS = 0x04C20
OFF_COLS = 0x04C60
OFF_VCOLS = 0x08660
OFF_SCOLS = 0x08760
OFF_LOTMP = 0x04E60
OFF_SCORR = 0x05660
OFF_SHCORR = 0x05E60
OFF_JUNK = 0x06660
OFF_HIF = 0x07660
OFF_XNT = 0x08800
OFF_GMIXT = 0x08800
OFF_HT = 0x10800
OFF_XT0 = 0x10800
OFF_XNN0 = 0x1C800
OFF_S = 0x10800
OFF_XCB = 0x20800
OFF_PAD = 0x2C800
OFF_WMB = 0x30C00
OFF_XT6 = 0x20800
OFF_OST = 0x24800
OFF_WGB = 0x31800
OFF_WUB = 0x33800
OFF_WDR = 0x31800
OFF_T1 = 0x35800
WDR_SLOTS = 6


class Pg:

    def __init__(self, nc):
        self.nc = nc
        self.sems = {}
        self.cnt = {}

    def sem(self, name):
        self.sems[name] = self.nc.alloc_semaphore(name)
        self.cnt[name] = 0

    def inc(self, ins, name, n=1):
        ins.then_inc(self.sems[name], n)
        self.cnt[name] += n
        return self.cnt[name]

    def dma(self, eng, name, out, in_):
        ins = eng.dma_start(out=out, in_=in_)
        return self.inc(ins, name, 16)

    def wait(self, eng, name, v):
        if v > 0:
            eng.wait_ge(self.sems[name], v)

    def c(self, name):
        return self.cnt[name]


def build_program(plans):
    nc = bass.Bass("TRN2", target_bir_lowering=False, enable_partition_id=True)

    xown = nc.dram_tensor("xown", [BATCH, SLAB, D], F32, kind="ExternalInput").ap()
    xfull = nc.dram_tensor("xfull", [BATCH, SEQ, D], F32, kind="ExternalInput").ap()
    wg = nc.dram_tensor("wg", [IM, 128, HT * 128], I32, kind="ExternalInput").ap()
    wu = nc.dram_tensor("wu", [IM, 128, HT * 128], I32, kind="ExternalInput").ap()
    wd = nc.dram_tensor("wd", [HM, 128, IM * 128], I32, kind="ExternalInput").ap()
    sg_in = nc.dram_tensor("sg_in", [128, IM], F32, kind="ExternalInput").ap()
    su_in = nc.dram_tensor("su_in", [128, IM], F32, kind="ExternalInput").ap()
    sd_in = nc.dram_tensor("sd_in", [128, HM], F32, kind="ExternalInput").ap()
    wmix = nc.dram_tensor("wmix", [128, 128, 512], F32, kind="ExternalInput").ap()
    y = nc.dram_tensor("y", [BATCH, SLAB, D], F32, kind="ExternalOutput").ap()

    at = nc.alloc_sbuf_tensor_at
    ident = at("ident", [128, 128], BF16, offset=OFF_IDENT)
    ones_col = at("ones_col", [128, 1], BF16, offset=OFF_ONESC)
    ones2 = at("ones2", [2, 128], BF16, offset=OFF_ONES2)
    sgs = at("sgs", [128, IM], F32, offset=OFF_SGS)
    sus = at("sus", [128, IM], F32, offset=OFF_SUS)
    sds = at("sds", [128, HM], F32, offset=OFF_SDS)
    colt = at("colt", [128, 16], F32, offset=OFF_COLS)
    lo_tmp = at("lo_tmp", [1, 1024], BF16, offset=OFF_LOTMP)
    scorr = at("scorr", [2, 1024], BF16, offset=OFF_SCORR)
    shcorr = at("shcorr", [2, 1024], BF16, offset=OFF_SHCORR)
    junk = at("junk", [128, 2048], BF16, offset=OFF_JUNK)
    hi_f = at("hi_f", [1, 1024], F32, offset=OFF_HIF)
    row_tmp = at("row_tmp", [1, 1024], F32, offset=OFF_JUNK)
    xnT = at("xnT", [128, HT, 1024], BF16, offset=OFF_XNT)
    gmixT = at("gmixT", [128, HM, 1024], BF16, offset=OFF_GMIXT)
    hT = at("hT", [128, IM, 1024], BF16, offset=OFF_HT)
    xt = [at(f"xt{i}", [128, 2048], F32, offset=OFF_XT0 + i * 0x2000) for i in range(6)]
    xnn_f = [at(f"xnnf{i}", [128, 2048], F32, offset=OFF_XNN0 + i * 0x2000) for i in range(2)]
    xnn = [at(f"xnn{i}", [128, 2048], BF16, offset=OFF_XNN0 + 0x4000 + i * 0x1000) for i in range(2)]
    padf = [at(f"padf{i}", [128, 2048], F32, offset=0x32800 + i * 0x2000) for i in range(2)]
    s_acc = at("s_acc", [128, NTILE, 2048], F32, offset=OFF_S)
    xcb = [at(f"xcb{i}", [128, 2048], F32, offset=OFF_XCB + i * 0x2000) for i in range(6)]
    pad = [at(f"pad{i}", [128, 2080], BF16, offset=OFF_PAD + i * 0x1080) for i in range(4)]
    wmb = [at(f"wmb{i}", [128, 512], BF16, offset=OFF_WMB + i * 0x400) for i in range(6)]
    vcols = at("vcols", [128, 64], F32, offset=OFF_VCOLS)
    scols = at("scols", [128, 32], F32, offset=OFF_SCOLS)
    xt6 = [at(f"xt6{i}", [128, 2048], F32, offset=OFF_XT6 + i * 0x2000) for i in range(2)]
    ost = [at(f"ost{i}", [128, 2048], F32, offset=OFF_OST + i * 0x2000) for i in range(2)]
    wgb = [at(f"wgb{i}", [128, 2048], BF16, offset=OFF_WGB + i * 0x1000) for i in range(2)]
    wub = [at(f"wub{i}", [128, 2048], BF16, offset=OFF_WUB + i * 0x1000) for i in range(2)]
    wdr = [at(f"wdr{i}", [128, 2048], BF16, offset=OFF_WDR + i * 0x1000) for i in range(WDR_SLOTS)]
    t1 = [at(f"t1_{i}", [128, 1024], F32, offset=OFF_T1 + i * 0x1000) for i in range(2)]

    p = Pg(nc)
    sems = ["qSc", "qXT0", "qXT1", "qCorr", "qS0", "qS1", "gEm",
            "qWG0", "qWG1", "qWU0", "qWU1",
            "tMM", "vE", "sE", "gE",
            "tMMm", "vEm", "sEm", "mph",
            "qXC0", "qXC1", "qWM0", "qWM1"]
    sems += [f"qWD{i}" for i in range(WDR_SLOTS)]
    sems += [f"qXTs{i}" for i in range(6)]
    sems += [f"qXCs{i}" for i in range(6)]
    sems += [f"qWMs{i}" for i in range(6)]
    for name in sems:
        p.sem(name)

    V, S, T, G, SY = nc.vector, nc.scalar, nc.tensor, nc.gpsimd, nc.sync

    for _h in p.sems.values():
        nc.gpsimd.sem_clear(_h)
    nc.all_engine_barrier()

    def col(k):
        return colt[:, k:k + 1]

    C_SUM, C_M, C_VT, C_V1 = 0, 1, 2, 3
    C_SQ, C_SD, C_SFEN = 4, 5, 6
    C_VE, C_RS, C_NB, C_VFEN = 8, 9, 10, 11

    import ml_dtypes as _mld
    ident_dram = nc.inline_tensor(
        np.eye(128, dtype=_mld.bfloat16), name="ident_const").ap()
    p.dma(SY, "qSc", ident[:], ident_dram)

    p.inc(V.memset(ones_col[:], 1.0), "vE")
    p.inc(V.memset(ones2[:], 1.0), "vE")
    ones_vE = p.c("vE")

    p.dma(SY, "qSc", sgs[:], sg_in)
    p.dma(SY, "qSc", sus[:], su_in)
    p.dma(SY, "qSc", sds[:], sd_in)
    scales_qX = p.c("qSc")
    ident_gE = scales_qX

    def xown_tile(i):
        n, s0 = i // (SLAB // 128), (i * 128) % SLAB
        return xown[n, s0:s0 + 128, :]

    nvops = [0]

    def vdummy():
        V.tensor_copy(vcols[:, 62:63], vcols[:, 63:64])
        nvops[0] += 1

    def vpad(target):
        while nvops[0] < target:
            vdummy()

    def vop(ins):
        nvops[0] += 1
        return ins

    def vc(i, k):
        return vcols[:, 6 * i + k:6 * i + k + 1]

    def scc(i, k):
        return scols[:, 2 * i + k:2 * i + k + 1]

    V_SUM, V_NM, V_VT, V_V1, V_VE, V_RS = range(6)
    S_SQ, S_SD = 0, 1

    with nc.psum_tensor("ptr0", [128, 1024], BF16) as ptr0, \
         nc.psum_tensor("ptr1", [128, 1024], BF16) as ptr1, \
         nc.psum_tensor("psS", [1, 1024], F32) as psS:
        ptr = [ptr0, ptr1]
        tr_tMM = [0] * NTILE
        xnt_sE = [0] * NTILE
        xnt_half_sE = [[0, 0] for _ in range(NTILE)]
        xtile_q = [0] * NTILE
        sq2_sE = [0] * NTILE
        sd_sE = [0] * NTILE
        rs_vop = [0] * NTILE
        apply_vE = [0] * NTILE
        V.memset(vcols[:, 62:63], 0.0)
        V.memset(vcols[:, 63:64], 0.0)
        for step in range(NTILE + 4):
            if step < NTILE:
                i = step
                sl = i % 6
                qn = f"qXTs{sl}"
                if i >= 6:
                    p.wait(SY, "vE", apply_vE[i - 6])
                    p.wait(SY, "sE", sq2_sE[i - 6])
                xtile_q[i] = p.dma(SY, qn, xt[sl][:], xown_tile(i))
                p.wait(V, qn, xtile_q[i])
                vop(V.reduce_sum(vc(i, V_SUM), xt[sl][:], axis=AX.X))
                p.wait(S, qn, xtile_q[i])
                ins = S.activation(junk[:], xt[sl][:], AF.Square,
                                   accum_out=scc(i, S_SQ))
                sq2_sE[i] = p.inc(ins, "sE")
            if step >= 2 and step - 2 < NTILE:
                j = step - 2
                vop(V.tensor_scalar_mul(vc(j, V_NM), vc(j, V_SUM), -1.0 / D))
                p.wait(V, "sE", sq2_sE[j])
                vop(V.tensor_scalar(vc(j, V_V1), scc(j, S_SQ), 1.0 / D, EPS,
                                    ALU.mult, ALU.add))
                vop(V.tensor_mul(vc(j, V_VT), vc(j, V_NM), vc(j, V_NM)))
                vdummy()
                vdummy()
                ins = vop(V.tensor_sub(vc(j, V_VE), vc(j, V_V1), vc(j, V_VT)))
                ve_vE = p.inc(ins, "vE")
                p.wait(S, "vE", ve_vE)
                ins = S.activation(scc(j, S_SD), vc(j, V_VE), AF.Sqrt)
                sd_sE[j] = p.inc(ins, "sE")
                p.wait(V, "sE", sd_sE[j])
                vop(V.reciprocal(vc(j, V_RS), scc(j, S_SD)))
                rs_vop[j] = nvops[0]
            if step >= 4:
                k = step - 4
                sl = k % 6
                parn = k % 2
                vpad(rs_vop[k] + 8)
                if k >= 2:
                    p.wait(V, "tMM", tr_tMM[k - 2])
                ins = vop(V.tensor_scalar(xnn_f[parn][:], xt[sl][:],
                                          vc(k, V_NM), vc(k, V_RS),
                                          ALU.add, ALU.mult))
                apply_vE[k] = p.inc(ins, "vE")
                p.wait(S, "vE", apply_vE[k])
                ins = S.copy(xnn[parn][:], xnn_f[parn][:])
                cast_sE = p.inc(ins, "sE")

                p.wait(T, "qSc", ident_gE)
                p.wait(T, "sE", cast_sE)
                for half in range(2):
                    if k >= 1:
                        p.wait(T, "sE", xnt_half_sE[k - 1][half])
                    for jj in range(8):
                        h = half * 8 + jj
                        ins = T.transpose(ptr[half][:, jj * 128:(jj + 1) * 128],
                                          xnn[parn][:, h * 128:(h + 1) * 128],
                                          ident[:])
                    tr_half = p.inc(ins, "tMM")
                    if half == 1:
                        tr_tMM[k] = tr_half
                    p.wait(S, "tMM", tr_half)
                    ins = S.copy(xnT[:, half * 8:(half + 1) * 8,
                                     k * 128:(k + 1) * 128],
                                 ptr[half][:].rearrange("p (a b) -> p a b", a=8))
                    xnt_half_sE[k][half] = p.inc(ins, "sE")
                    xnt_sE[k] = xnt_half_sE[k][half]
        all_xnT_sE = p.c("sE")

        p.wait(T, "sE", all_xnT_sE)
        p.wait(T, "vE", ones_vE)
        for c in range(2):
            for h in range(HT):
                ins = T.matmul(psS[:, c * 512:(c + 1) * 512], ones_col[:],
                               xnT[:, h, c * 512:(c + 1) * 512],
                               start=(h == 0), stop=(h == HT - 1))
            p.inc(ins, "tMM")
        srow_tMM = p.c("tMM")
        p.wait(S, "tMM", srow_tMM)
        ins = S.activation(scorr[0:1, :], psS[:], AF.Copy, scale=-128.0)
        hi_sE = p.inc(ins, "sE")
        p.wait(V, "sE", hi_sE)
        V.tensor_scalar_mul(row_tmp[:], psS[:], -128.0)
        V.tensor_copy(hi_f[:], scorr[0:1, :])
        vdummy()
        vdummy()
        ins = V.tensor_sub(lo_tmp[:], row_tmp[:], hi_f[:])
        lo_vE = p.inc(ins, "vE")
        p.wait(SY, "vE", lo_vE)
        scorr_qX = p.dma(SY, "qCorr", scorr[1:2, :], lo_tmp[:])
        scorr_sE = hi_sE

    with nc.psum_tensor("pg0", [128, 512]) as pg0, \
         nc.psum_tensor("pg1", [128, 512]) as pg1, \
         nc.psum_tensor("pg2", [128, 512]) as pg2, \
         nc.psum_tensor("pg3", [128, 512]) as pg3, \
         nc.psum_tensor("pg4", [128, 512]) as pg4, \
         nc.psum_tensor("pg5", [128, 512]) as pg5, \
         nc.psum_tensor("psSh", [1, 1024], F32) as psSh:
        pg = [pg0, pg1, pg2, pg3, pg4, pg5]
        wdma_qW = [0] * IM
        chain_done_tMM = [0] * IM
        evac_sE = {}
        evac_vE = {}
        stt_vE = [0] * IM
        silu_sE = [0] * IM

        def emit_wdma(m):
            if m >= 2:
                p.wait(G, "tMM", chain_done_tMM[m - 2])
            gq, uq = f"qWG{m % 2}", f"qWU{m % 2}"
            p.dma(G, gq, wgb[m % 2][:], wg[m])
            p.dma(G, uq, wub[m % 2][:], wu[m])
            wdma_qW[m] = (gq, p.c(gq), uq, p.c(uq))

        emit_wdma(0)
        emit_wdma(1)

        first_mm = True
        for m in range(IM):
            par = m % 2
            ch0 = 4 * m
            if first_mm:
                p.wait(T, "sE", scorr_sE)
                p.wait(T, "qCorr", scorr_qX)
                p.wait(T, "vE", ones_vE)
                first_mm = False
            gq, gv, uq, uv = wdma_qW[m]
            p.wait(T, gq, gv)
            p.wait(T, uq, uv)
            for j in range(4):
                chp = ch0 + j - 6
                if chp >= 0:
                    if chp in evac_sE:
                        p.wait(T, "sE", evac_sE.pop(chp))
                    else:
                        p.wait(T, "vE", evac_vE.pop(chp))
            bg = [pg[(ch0 + 0) % 6], pg[(ch0 + 1) % 6]]
            bu = [pg[(ch0 + 2) % 6], pg[(ch0 + 3) % 6]]
            for h in range(HT):
                for c in range(2):
                    T.matmul(bg[c][:], wgb[par][:, h * 128:(h + 1) * 128],
                             xnT[:, h, c * 512:(c + 1) * 512],
                             start=(h == 0), stop=False)
            g_tMM = [0, 0]
            for c in range(2):
                ins = T.matmul(bg[c][:], ones2[:], scorr[:, c * 512:(c + 1) * 512],
                               start=False, stop=True)
                g_tMM[c] = p.inc(ins, "tMM")
            for h in range(HT):
                for c in range(2):
                    T.matmul(bu[c][:], wub[par][:, h * 128:(h + 1) * 128],
                             xnT[:, h, c * 512:(c + 1) * 512],
                             start=(h == 0), stop=False)
            u_tMM = [0, 0]
            for c in range(2):
                ins = T.matmul(bu[c][:], ones2[:], scorr[:, c * 512:(c + 1) * 512],
                               start=False, stop=True)
                u_tMM[c] = p.inc(ins, "tMM")
            chain_done_tMM[m] = p.c("tMM")
            if m + 2 < IM:
                emit_wdma(m + 2)

            if m == 0:
                p.wait(S, "qSc", scales_qX)
                p.wait(V, "qSc", scales_qX)
            if m >= 2:
                p.wait(S, "vE", stt_vE[m - 2])
            sg_col = sgs[:, m:m + 1]
            s_cnt = [0, 0]
            for c in range(2):
                p.wait(S, "tMM", g_tMM[c])
                ins = S.activation(t1[par][:, c * 512:(c + 1) * 512], bg[c][:],
                                   AF.Silu, scale=sg_col)
                s_cnt[c] = p.inc(ins, "sE")
            evac_sE[ch0 + 0] = s_cnt[0]
            evac_sE[ch0 + 1] = s_cnt[1]
            silu_sE[m] = s_cnt[1]
            su_col = sus[:, m:m + 1]
            v_cnt = [0, 0]
            for c in range(2):
                p.wait(V, "tMM", u_tMM[c])
                p.wait(V, "sE", s_cnt[c])
                ins = V.scalar_tensor_tensor(hT[:, m, c * 512:(c + 1) * 512],
                                             bu[c][:], su_col,
                                             t1[par][:, c * 512:(c + 1) * 512],
                                             ALU.mult, ALU.mult)
                v_cnt[c] = p.inc(ins, "vE")
            evac_vE[ch0 + 2] = v_cnt[0]
            evac_vE[ch0 + 3] = v_cnt[1]
            stt_vE[m] = v_cnt[1]

            if m >= 2:
                ms = m - 2
                p.wait(T, "vE", stt_vE[ms])
                for c in range(2):
                    ins = T.matmul(psSh[:, c * 512:(c + 1) * 512], ones_col[:],
                                   hT[:, ms, c * 512:(c + 1) * 512],
                                   start=(ms == 0), stop=False)
        for ms in (IM - 2, IM - 1):
            p.wait(T, "vE", stt_vE[ms])
            for c in range(2):
                ins = T.matmul(psSh[:, c * 512:(c + 1) * 512], ones_col[:],
                               hT[:, ms, c * 512:(c + 1) * 512],
                               start=False, stop=(ms == IM - 1))
        sh_tMM = p.inc(ins, "tMM")
        p.wait(S, "tMM", sh_tMM)
        ins = S.activation(shcorr[0:1, :], psSh[:], AF.Copy, scale=-128.0)
        shhi_sE = p.inc(ins, "sE")
        p.wait(V, "sE", shhi_sE)
        V.tensor_scalar_mul(row_tmp[:], psSh[:], -128.0)
        V.tensor_copy(hi_f[:], shcorr[0:1, :])
        vdummy()
        vdummy()
        ins = V.tensor_sub(lo_tmp[:], row_tmp[:], hi_f[:])
        shlo_vE = p.inc(ins, "vE")
        p.wait(SY, "vE", shlo_vE)
        shcorr_qX = p.dma(SY, "qCorr", shcorr[1:2, :], lo_tmp[:])
        all_stt_vE = p.c("vE")
        all_silu_sE = max(silu_sE)

    with nc.psum_tensor("pd0", [128, 512]) as pd0, \
         nc.psum_tensor("pd1", [128, 512]) as pd1, \
         nc.psum_tensor("pd2", [128, 512]) as pd2, \
         nc.psum_tensor("pd3", [128, 512]) as pd3:
        pd = [pd0, pd1, pd2, pd3]
        sub_qW = {}
        sub_tMM = {}
        gm_sE = {}
        kseq = 0
        def emit_wd_dma(m, q):
            k = 4 * m + q
            if k >= WDR_SLOTS:
                p.wait(G, "tMM", sub_tMM[k - WDR_SLOTS])
            qd = f"qWD{k % WDR_SLOTS}"
            v = p.dma(G, qd, wdr[k % WDR_SLOTS][:],
                      wd[m, :, q * 2048:(q + 1) * 2048])
            sub_qW[(m, q)] = (qd, v)

        p.wait(G, "tMM", chain_done_tMM[IM - 1])
        p.wait(G, "vE", all_stt_vE)
        p.wait(G, "sE", all_silu_sE)
        for k0 in range(WDR_SLOTS):
            emit_wd_dma(k0 // 4, k0 % 4)

        first_mm = True
        for m in range(HM):
            ch0 = 2 * m
            if first_mm:
                p.wait(T, "vE", all_stt_vE)
                p.wait(T, "sE", all_silu_sE)
                p.wait(T, "tMM", sh_tMM)
                p.wait(T, "sE", shhi_sE)
                p.wait(T, "qCorr", shcorr_qX)
                first_mm = False
            for j in range(2):
                chp = ch0 + j - 4
                if chp >= 0:
                    p.wait(T, "sE", gm_sE.pop(chp))
            b0, b1 = pd[(ch0 + 0) % 4], pd[(ch0 + 1) % 4]
            for u in range(IM):
                q = u // 16
                if u % 16 == 0:
                    qd, vd = sub_qW[(m, q)]
                    p.wait(T, qd, vd)
                slot = (4 * m + q) % WDR_SLOTS
                lh = wdr[slot][:, (u % 16) * 128:(u % 16 + 1) * 128]
                T.matmul(b0[:], lh, hT[:, u, 0:512], start=(u == 0), stop=False)
                ins = T.matmul(b1[:], lh, hT[:, u, 512:1024], start=(u == 0), stop=False)
                if u % 16 == 15:
                    k = 4 * m + q
                    sub_tMM[k] = p.inc(ins, "tMM")
                    nk = k + WDR_SLOTS
                    if nk < 4 * HM:
                        emit_wd_dma(nk // 4, nk % 4)
            d_tMM = [0, 0]
            for c, b in ((0, b0), (1, b1)):
                ins = T.matmul(b[:], ones2[:], shcorr[:, c * 512:(c + 1) * 512],
                               start=False, stop=True)
                d_tMM[c] = p.inc(ins, "tMM")
            sd_col = sds[:, m:m + 1]
            for c, b in ((0, b0), (1, b1)):
                if m == 0 and c == 0:
                    p.wait(S, "qSc", scales_qX)
                p.wait(S, "tMM", d_tMM[c])
                ins = S.activation(gmixT[:, m, c * 512:(c + 1) * 512], b[:],
                                   AF.Silu, scale=sd_col)
                gm_sE[ch0 + c] = p.inc(ins, "sE")
        gmix_done_sE = p.c("sE")
        endP4_tMM = p.c("tMM")

    with nc.psum_tensor("pm0", [64, 512]) as pm0, \
         nc.psum_tensor("pm1", [64, 512]) as pm1, \
         nc.psum_tensor("pm2", [64, 512]) as pm2, \
         nc.psum_tensor("pm3", [64, 512]) as pm3, \
         nc.psum_tensor("pm4", [64, 512]) as pm4, \
         nc.psum_tensor("pm5", [64, 512]) as pm5, \
         nc.psum_tensor("pm6", [64, 512]) as pm6, \
         nc.psum_tensor("pm7", [64, 512]) as pm7:
        pm = [pm0, pm1, pm2, pm3, pm4, pm5, pm6, pm7]
        engines = [nc.tensor, nc.vector, nc.scalar, nc.gpsimd, nc.sync]
        idx = [e.partition_id() for e in engines]
        for core in nc.Switch(engines=engines, index=idx, n=NCORE):
            groups = plans[core]
            for nm in (["tMMm", "vEm", "sEm", "gEm"]
                       + [f"qXCs{sl}" for sl in range(6)]
                       + [f"qWMs{sl}" for sl in range(6)]):
                p.cnt[nm] = 0

            p.wait(V, "tMM", endP4_tMM)
            ins = V.memset(s_acc[:], 0.0)
            memset_vEm = p.inc(ins, "vEm")
            p.wait(G, "tMM", endP4_tMM)
            for _ps in range(4):
                ins = G.memset(pad[_ps][:], 0.0)
            padz_gEm = p.inc(ins, "gEm")

            insts = []
            for gidx, (r_loc, n, bs) in enumerate(groups):
                for ci, b in enumerate(bs):
                    insts.append((b, n, gidx, r_loc, ci == 0, ci == len(bs) - 1))
            K = len(insts)
            wq_l = [None] * K
            padf_vEm = [0] * K
            padcp = [0] * K
            sq2m = [0] * K
            rs_vopm = [0] * K
            applym = [0] * K
            mm_done = [0] * K
            grp_evac = {}
            first_T = True
            for t in range(K + 4):
                if t < K:
                    b, n, gidx, r_loc, gfirst, glast = insts[t]
                    sl = t % 6
                    colv = int(plans["cols"][b])
                    qc = f"qXCs{sl}"
                    if t == 0:
                        p.wait(SY, "tMM", endP4_tMM)
                    if t >= 6:
                        p.wait(SY, "vEm", padf_vEm[t - 6])
                        p.wait(SY, "sEm", sq2m[t - 6])
                    src = xfull[n, 64 * colv:64 * colv + 64, :]
                    p.dma(SY, qc, xcb[sl][0:64, :], src)
                    ld_q = p.dma(SY, qc, xcb[sl][64:128, :], src)
                    qm = f"qWMs{sl}"
                    if t == 0:
                        p.wait(G, "tMM", endP4_tMM)
                    if t >= 6:
                        p.wait(G, "tMMm", mm_done[t - 6])
                    wq_l[t] = (qm, p.dma(G, qm, wmb[sl][:], wmix[b]))
                    cs = t % 4
                    p.wait(V, qc, ld_q)
                    vop(V.reduce_sum(vc(cs, V_SUM), xcb[sl][:], axis=AX.X))
                    p.wait(S, qc, ld_q)
                    ins = S.activation(junk[:], xcb[sl][:], AF.Square,
                                       accum_out=scc(cs, S_SQ))
                    sq2m[t] = p.inc(ins, "sEm")
                if t >= 2 and t - 2 < K:
                    j = t - 2
                    cs = j % 4
                    vop(V.tensor_scalar_mul(vc(cs, V_NM), vc(cs, V_SUM), -1.0 / D))
                    p.wait(V, "sEm", sq2m[j])
                    vop(V.tensor_scalar(vc(cs, V_V1), scc(cs, S_SQ), 1.0 / D, EPS,
                                        ALU.mult, ALU.add))
                    vop(V.tensor_mul(vc(cs, V_VT), vc(cs, V_NM), vc(cs, V_NM)))
                    vdummy()
                    vdummy()
                    ins = vop(V.tensor_sub(vc(cs, V_VE), vc(cs, V_V1), vc(cs, V_VT)))
                    ve_v = p.inc(ins, "vEm")
                    p.wait(S, "vEm", ve_v)
                    ins = S.activation(scc(cs, S_SD), vc(cs, V_VE), AF.Sqrt)
                    sd_s = p.inc(ins, "sEm")
                    p.wait(V, "sEm", sd_s)
                    vop(V.reciprocal(vc(cs, V_RS), scc(cs, S_SD)))
                    rs_vopm[j] = nvops[0]
                if t >= 4:
                    k = t - 4
                    b, n, gidx, r_loc, gfirst, glast = insts[k]
                    sl = k % 6
                    ps = k % 4
                    cs = k % 4
                    pf = k % 2
                    vpad(rs_vopm[k] + 8)
                    if k >= 2:
                        p.wait(V, "sEm", padcp[k - 2])
                    ins = vop(V.tensor_scalar(padf[pf][:], xcb[sl][:],
                                              vc(cs, V_NM), vc(cs, V_RS),
                                              ALU.add, ALU.mult))
                    padf_vEm[k] = p.inc(ins, "vEm")
                    p.wait(S, "vEm", padf_vEm[k])
                    if k < 4:
                        p.wait(S, "gEm", padz_gEm)
                    else:
                        p.wait(S, "tMMm", mm_done[k - 4])
                    S.copy(pad[ps][0:64, 15:2063], padf[pf][0:64, :])
                    ins = S.copy(pad[ps][64:128, 14:2062], padf[pf][64:128, :])
                    padcp[k] = p.inc(ins, "sEm")
                    applym[k] = padcp[k]
                    banks = pm[4 * (gidx % 2):4 * (gidx % 2) + 4]
                    if first_T:
                        p.wait(T, "tMM", endP4_tMM)
                        p.wait(T, "sE", gmix_done_sE)
                        first_T = False
                    if gfirst and gidx >= 2:
                        p.wait(T, "sEm", grp_evac[gidx - 2])
                    p.wait(T, wq_l[k][0], wq_l[k][1])
                    p.wait(T, "sEm", applym[k])
                    ins = None
                    for jj in range(8):
                        lh = wmb[sl][:, jj * 64:(jj + 1) * 64]
                        for c in range(4):
                            ins = T.matmul(
                                banks[c][:], lh,
                                pad[ps][:, c * 512 + 2 * jj:c * 512 + 2 * jj + 512],
                                start=(gfirst and jj == 0),
                                stop=(glast and jj == 7))
                    mm_done[k] = p.inc(ins, "tMMm")
                    if glast:
                        tile_idx = n * (NTILE // 2) + r_loc // 2
                        p0 = (r_loc % 2) * 64
                        p.wait(S, "tMMm", mm_done[k])
                        p.wait(S, "vEm", memset_vEm)
                        ins = None
                        for c in range(4):
                            ins = S.copy(s_acc[p0:p0 + 64, tile_idx,
                                               c * 512:(c + 1) * 512],
                                         banks[c][0:64, :])
                        grp_evac[gidx] = p.inc(ins, "sEm")

            V.tensor_copy(col(C_VFEN), col(C_SUM)).then_inc(p.sems["mph"], 1)
            S.copy(col(C_SFEN), col(C_SQ)).then_inc(p.sems["mph"], 1)
            T.nop().then_inc(p.sems["mph"], 1)
            G.nop().then_inc(p.sems["mph"], 1)
            SY.nop().then_inc(p.sems["mph"], 1)
        p.cnt["mph"] = 5

    with nc.psum_tensor("pt60", [128, 2048], BF16) as pt60, \
         nc.psum_tensor("pt61", [128, 2048], BF16) as pt61:
        pt6 = [pt60, pt61]
        comb_vE = [0] * NTILE
        store_qS = [0] * NTILE
        tr6_tMM = [0] * NTILE
        for i in range(NTILE):
            par = i % 2
            if i == 0:
                p.wait(SY, "mph", 5)
                p.wait(T, "mph", 5)
                p.wait(V, "mph", 5)
                p.wait(S, "mph", 5)
            qn = "qXT0" if par == 0 else "qXT1"
            if i >= 2:
                p.wait(SY, "vE", comb_vE[i - 2])
            x_q = (qn, p.dma(SY, qn, xt6[par][:], xown_tile(i)))
            if i >= 2:
                p.wait(T, "vE", comb_vE[i - 2])
            ins = None
            for m in range(HM):
                ins = T.transpose(pt6[par][:, m * 128:(m + 1) * 128],
                                  gmixT[:, m, i * 128:(i + 1) * 128], ident[:])
            tr6_tMM[i] = p.inc(ins, "tMM")
            p.wait(V, x_q[0], x_q[1])
            p.wait(V, "tMM", tr6_tMM[i])
            if i >= 2:
                p.wait(V, store_qS[i - 2][0], store_qS[i - 2][1])
            V.tensor_tensor(ost[par][:], s_acc[:, i, :], pt6[par][:], ALU.mult)
            vdummy()
            vdummy()
            ins = V.tensor_add(ost[par][:], ost[par][:], xt6[par][:])
            comb_vE[i] = p.inc(ins, "vE")
            p.wait(S, "vE", comb_vE[i])
            n, s0 = i // (SLAB // 128), (i * 128) % SLAB
            qs = "qS0" if par == 0 else "qS1"
            store_qS[i] = (qs, p.dma(S, qs, y[n, s0:s0 + 128, :], ost[par][:]))
        p.wait(S, "qS0", p.c("qS0"))
        p.wait(S, "qS1", p.c("qS1"))
        p.wait(SY, "qS0", p.c("qS0"))
        p.wait(SY, "qS1", p.c("qS1"))

    return nc


def _plan(rows):
    plans = {}
    for c in range(NCORE):
        groups = []
        for r_loc in range(8):
            bs = [b for b in range(128) if int(rows[b]) == 8 * c + r_loc]
            if bs:
                for n in range(BATCH):
                    groups.append((r_loc, n, bs))
        plans[c] = groups
    return plans


def kernel(L0_attn_x, L0_attn_block_values, L0_attn_cols, L0_attn_rows,
           L0_ffn_gate_w, L0_ffn_gate_scale, L0_ffn_up_w, L0_ffn_up_scale,
           L0_ffn_down_w, L0_ffn_down_scale):
    x = np.ascontiguousarray(np.asarray(L0_attn_x, dtype=np.float32))
    blockv = np.asarray(L0_attn_block_values, dtype=np.float32)
    cols = np.asarray(L0_attn_cols, dtype=np.int32)
    rows = np.asarray(L0_attn_rows, dtype=np.int32)
    gw = np.asarray(L0_ffn_gate_w, dtype=np.int32)
    uw = np.asarray(L0_ffn_up_w, dtype=np.int32)
    dw = np.asarray(L0_ffn_down_w, dtype=np.int32)

    gp = np.ascontiguousarray(
        gw.reshape(IM, 128, HT, 128).transpose(0, 3, 2, 1)).reshape(IM, 128, HT * 128)
    up = np.ascontiguousarray(
        uw.reshape(IM, 128, HT, 128).transpose(0, 3, 2, 1)).reshape(IM, 128, HT * 128)
    dp = np.ascontiguousarray(
        dw.reshape(HM, 128, IM, 128).transpose(0, 3, 2, 1)).reshape(HM, 128, IM * 128)
    sgp = np.ascontiguousarray(
        np.asarray(L0_ffn_gate_scale, dtype=np.float32).reshape(IM, 128).T)
    sup = np.ascontiguousarray(
        np.asarray(L0_ffn_up_scale, dtype=np.float32).reshape(IM, 128).T)
    sdp = np.ascontiguousarray(
        np.asarray(L0_ffn_down_scale, dtype=np.float32).reshape(HM, 128).T)
    wm = np.ascontiguousarray(
        blockv.reshape(128, 64, 64, 8, 2).transpose(0, 4, 2, 3, 1)
    ).reshape(128, 128, 512)

    plans = _plan(rows)
    plans["cols"] = cols
    nc = build_program(plans)

    in_maps = []
    for c in range(NCORE):
        in_maps.append({
            "xown": np.ascontiguousarray(x[:, SLAB * c:SLAB * (c + 1), :]),
            "xfull": x,
            "wg": gp, "wu": up, "wd": dp,
            "sg_in": sgp, "su_in": sup, "sd_in": sdp,
            "wmix": wm,
        })
    res = run_bass_kernel_spmd(nc, in_maps, core_ids=list(range(NCORE)))
    out = np.empty((BATCH, SEQ, D), dtype=np.float32)
    for c in range(NCORE):
        out[:, SLAB * c:SLAB * (c + 1), :] = res.results[c]["y"]
    return out


# version 14
# speedup vs baseline: 1.0028x; 1.0023x over previous
import numpy as np
import concourse.bass as bass
import concourse.mybir as mybir
from concourse.bass_utils import run_bass_kernel_spmd

F32 = mybir.dt.float32
BF16 = mybir.dt.bfloat16
I32 = mybir.dt.int32
AF = mybir.ActivationFunctionType
ALU = mybir.AluOpType
AX = mybir.AxisListType

D = 2048
SEQ = 4096
BATCH = 2
NCORE = 8
SLAB = SEQ // NCORE
NTOK = BATCH * SLAB
NTILE = NTOK // 128
IM = 64
HM = 16
HT = 16
EPS = 1e-5

OFF_IDENT = 0x04800
OFF_ONESC = 0x04900
OFF_ONES2 = 0x04920
OFF_SGS = 0x04A20
OFF_SUS = 0x04B20
OFF_SDS = 0x04C20
OFF_COLS = 0x04C60
OFF_VCOLS = 0x08660
OFF_SCOLS = 0x08760
OFF_LOTMP = 0x04E60
OFF_SCORR = 0x05660
OFF_SHCORR = 0x05E60
OFF_JUNK = 0x06660
OFF_HIF = 0x07660
OFF_XNT = 0x08800
OFF_GMIXT = 0x08800
OFF_HT = 0x10800
OFF_XT0 = 0x10800
OFF_XNN0 = 0x1C800
OFF_S = 0x10800
OFF_XCB = 0x20800
OFF_PAD = 0x2C800
OFF_WMB = 0x30C00
OFF_XT6 = 0x20800
OFF_OST = 0x24800
OFF_WGB = 0x31800
OFF_WUB = 0x33800
OFF_WDR = 0x31800
OFF_T1 = 0x35800
WDR_SLOTS = 6


class Pg:

    def __init__(self, nc):
        self.nc = nc
        self.sems = {}
        self.cnt = {}

    def sem(self, name):
        self.sems[name] = self.nc.alloc_semaphore(name)
        self.cnt[name] = 0

    def inc(self, ins, name, n=1):
        ins.then_inc(self.sems[name], n)
        self.cnt[name] += n
        return self.cnt[name]

    def dma(self, eng, name, out, in_):
        ins = eng.dma_start(out=out, in_=in_)
        return self.inc(ins, name, 16)

    def wait(self, eng, name, v):
        if v > 0:
            eng.wait_ge(self.sems[name], v)

    def c(self, name):
        return self.cnt[name]


def build_program(plans):
    nc = bass.Bass("TRN2", target_bir_lowering=False, enable_partition_id=True)

    xown = nc.dram_tensor("xown", [BATCH, SLAB, D], F32, kind="ExternalInput").ap()
    xfull = nc.dram_tensor("xfull", [BATCH, SEQ, D], F32, kind="ExternalInput").ap()
    wg = nc.dram_tensor("wg", [IM, 128, HT * 128], I32, kind="ExternalInput").ap()
    wu = nc.dram_tensor("wu", [IM, 128, HT * 128], I32, kind="ExternalInput").ap()
    wd = nc.dram_tensor("wd", [HM, 128, IM * 128], I32, kind="ExternalInput").ap()
    sg_in = nc.dram_tensor("sg_in", [128, IM], F32, kind="ExternalInput").ap()
    su_in = nc.dram_tensor("su_in", [128, IM], F32, kind="ExternalInput").ap()
    sd_in = nc.dram_tensor("sd_in", [128, HM], F32, kind="ExternalInput").ap()
    wmix = nc.dram_tensor("wmix", [128, 128, 512], F32, kind="ExternalInput").ap()
    y = nc.dram_tensor("y", [BATCH, SLAB, D], F32, kind="ExternalOutput").ap()

    at = nc.alloc_sbuf_tensor_at
    ident = at("ident", [128, 128], BF16, offset=OFF_IDENT)
    ones_col = at("ones_col", [128, 1], BF16, offset=OFF_ONESC)
    ones2 = at("ones2", [2, 128], BF16, offset=OFF_ONES2)
    sgs = at("sgs", [128, IM], F32, offset=OFF_SGS)
    sus = at("sus", [128, IM], F32, offset=OFF_SUS)
    sds = at("sds", [128, HM], F32, offset=OFF_SDS)
    colt = at("colt", [128, 16], F32, offset=OFF_COLS)
    lo_tmp = at("lo_tmp", [1, 1024], BF16, offset=OFF_LOTMP)
    scorr = at("scorr", [2, 1024], BF16, offset=OFF_SCORR)
    shcorr = at("shcorr", [2, 1024], BF16, offset=OFF_SHCORR)
    junk = at("junk", [128, 2048], BF16, offset=OFF_JUNK)
    hi_f = at("hi_f", [1, 1024], F32, offset=OFF_HIF)
    row_tmp = at("row_tmp", [1, 1024], F32, offset=OFF_JUNK)
    xnT = at("xnT", [128, HT, 1024], BF16, offset=OFF_XNT)
    gmixT = at("gmixT", [128, HM, 1024], BF16, offset=OFF_GMIXT)
    hT = at("hT", [128, IM, 1024], BF16, offset=OFF_HT)
    xt = [at(f"xt{i}", [128, 2048], F32, offset=OFF_XT0 + i * 0x2000) for i in range(6)]
    xnn_f = [at(f"xnnf{i}", [128, 2048], F32, offset=OFF_XNN0 + i * 0x2000) for i in range(2)]
    xnn = [at(f"xnn{i}", [128, 2048], BF16, offset=OFF_XNN0 + 0x4000 + i * 0x1000) for i in range(2)]
    padf = [at(f"padf{i}", [128, 2048], F32, offset=0x32800 + i * 0x2000) for i in range(2)]
    s_acc = at("s_acc", [128, NTILE, 2048], F32, offset=OFF_S)
    xcb = [at(f"xcb{i}", [128, 2048], F32, offset=OFF_XCB + i * 0x2000) for i in range(6)]
    pad = [at(f"pad{i}", [128, 2080], BF16, offset=OFF_PAD + i * 0x1080) for i in range(4)]
    wmb = [at(f"wmb{i}", [128, 512], BF16, offset=OFF_WMB + i * 0x400) for i in range(6)]
    vcols = at("vcols", [128, 64], F32, offset=OFF_VCOLS)
    scols = at("scols", [128, 32], F32, offset=OFF_SCOLS)
    xt6 = [at(f"xt6{i}", [128, 2048], F32, offset=OFF_XT6 + i * 0x2000) for i in range(2)]
    ost = [at(f"ost{i}", [128, 2048], F32, offset=OFF_OST + i * 0x2000) for i in range(2)]
    wgb = [at(f"wgb{i}", [128, 2048], BF16, offset=OFF_WGB + i * 0x1000) for i in range(2)]
    wub = [at(f"wub{i}", [128, 2048], BF16, offset=OFF_WUB + i * 0x1000) for i in range(2)]
    wdr = [at(f"wdr{i}", [128, 2048], BF16, offset=OFF_WDR + i * 0x1000) for i in range(WDR_SLOTS)]
    t1 = [at(f"t1_{i}", [128, 1024], F32, offset=OFF_T1 + i * 0x1000) for i in range(2)]

    p = Pg(nc)
    sems = ["qSc", "qXT0", "qXT1", "qCorr", "qS0", "qS1", "gEm",
            "qWG0", "qWG1", "qWU0", "qWU1",
            "tMM", "vE", "sE", "gE",
            "tMMm", "vEm", "sEm", "mph",
            "qXC0", "qXC1", "qWM0", "qWM1"]
    sems += [f"qWD{i}" for i in range(WDR_SLOTS)]
    sems += [f"qXTs{i}" for i in range(6)]
    sems += [f"qXCs{i}" for i in range(6)]
    sems += [f"qWMs{i}" for i in range(6)]
    for name in sems:
        p.sem(name)

    V, S, T, G, SY = nc.vector, nc.scalar, nc.tensor, nc.gpsimd, nc.sync

    for _h in p.sems.values():
        nc.gpsimd.sem_clear(_h)
    nc.all_engine_barrier()

    def col(k):
        return colt[:, k:k + 1]

    C_SUM, C_M, C_VT, C_V1 = 0, 1, 2, 3
    C_SQ, C_SD, C_SFEN = 4, 5, 6
    C_VE, C_RS, C_NB, C_VFEN = 8, 9, 10, 11

    import ml_dtypes as _mld
    ident_dram = nc.inline_tensor(
        np.eye(128, dtype=_mld.bfloat16), name="ident_const").ap()
    p.dma(SY, "qSc", ident[:], ident_dram)

    p.inc(V.memset(ones_col[:], 1.0), "vE")
    p.inc(V.memset(ones2[:], 1.0), "vE")
    ones_vE = p.c("vE")

    p.dma(SY, "qSc", sgs[:], sg_in)
    p.dma(SY, "qSc", sus[:], su_in)
    p.dma(SY, "qSc", sds[:], sd_in)
    scales_qX = p.c("qSc")
    ident_gE = scales_qX

    def xown_tile(i):
        n, s0 = i // (SLAB // 128), (i * 128) % SLAB
        return xown[n, s0:s0 + 128, :]

    nvops = [0]

    def vdummy():
        V.engine_nop()
        nvops[0] += 1

    def vpad(target):
        while nvops[0] < target:
            vdummy()

    def vop(ins):
        nvops[0] += 1
        return ins

    def vc(i, k):
        return vcols[:, 6 * i + k:6 * i + k + 1]

    def scc(i, k):
        return scols[:, 2 * i + k:2 * i + k + 1]

    V_SUM, V_NM, V_VT, V_V1, V_VE, V_RS = range(6)
    S_SQ, S_SD = 0, 1

    with nc.psum_tensor("ptr0", [128, 1024], BF16) as ptr0, \
         nc.psum_tensor("ptr1", [128, 1024], BF16) as ptr1, \
         nc.psum_tensor("psS", [1, 1024], F32) as psS:
        ptr = [ptr0, ptr1]
        tr_tMM = [0] * NTILE
        xnt_sE = [0] * NTILE
        xnt_half_sE = [[0, 0] for _ in range(NTILE)]
        xtile_q = [0] * NTILE
        sq2_sE = [0] * NTILE
        sd_sE = [0] * NTILE
        rs_vop = [0] * NTILE
        apply_vE = [0] * NTILE
        for step in range(NTILE + 4):
            if step < NTILE:
                i = step
                sl = i % 6
                qn = f"qXTs{sl}"
                if i >= 6:
                    p.wait(SY, "vE", apply_vE[i - 6])
                    p.wait(SY, "sE", sq2_sE[i - 6])
                xtile_q[i] = p.dma(SY, qn, xt[sl][:], xown_tile(i))
                p.wait(V, qn, xtile_q[i])
                vop(V.reduce_sum(vc(i, V_SUM), xt[sl][:], axis=AX.X))
                p.wait(S, qn, xtile_q[i])
                if i >= 1:
                    p.wait(S, "sE", sq2_sE[i - 1])
                ins = S.activation(junk[:], xt[sl][:], AF.Square,
                                   accum_out=scc(i, S_SQ))
                sq2_sE[i] = p.inc(ins, "sE")
            if step >= 2 and step - 2 < NTILE:
                j = step - 2
                vop(V.tensor_scalar_mul(vc(j, V_NM), vc(j, V_SUM), -1.0 / D))
                p.wait(V, "sE", sq2_sE[j])
                vop(V.tensor_scalar(vc(j, V_V1), scc(j, S_SQ), 1.0 / D, EPS,
                                    ALU.mult, ALU.add))
                vop(V.tensor_mul(vc(j, V_VT), vc(j, V_NM), vc(j, V_NM)))
                vdummy()
                vdummy()
                ins = vop(V.tensor_sub(vc(j, V_VE), vc(j, V_V1), vc(j, V_VT)))
                ve_vE = p.inc(ins, "vE")
                p.wait(S, "vE", ve_vE)
                ins = S.activation(scc(j, S_SD), vc(j, V_VE), AF.Sqrt)
                sd_sE[j] = p.inc(ins, "sE")
                p.wait(V, "sE", sd_sE[j])
                vop(V.reciprocal(vc(j, V_RS), scc(j, S_SD)))
                rs_vop[j] = nvops[0]
            if step >= 4:
                k = step - 4
                sl = k % 6
                parn = k % 2
                vpad(rs_vop[k] + 8)
                if k >= 2:
                    p.wait(V, "tMM", tr_tMM[k - 2])
                ins = vop(V.tensor_scalar(xnn_f[parn][:], xt[sl][:],
                                          vc(k, V_NM), vc(k, V_RS),
                                          ALU.add, ALU.mult))
                apply_vE[k] = p.inc(ins, "vE")
                p.wait(S, "vE", apply_vE[k])
                ins = S.copy(xnn[parn][:], xnn_f[parn][:])
                cast_sE = p.inc(ins, "sE")

                p.wait(T, "qSc", ident_gE)
                p.wait(T, "sE", cast_sE)
                for half in range(2):
                    if k >= 1:
                        p.wait(T, "sE", xnt_half_sE[k - 1][half])
                    for jj in range(8):
                        h = half * 8 + jj
                        ins = T.transpose(ptr[half][:, jj * 128:(jj + 1) * 128],
                                          xnn[parn][:, h * 128:(h + 1) * 128],
                                          ident[:])
                    tr_half = p.inc(ins, "tMM")
                    if half == 1:
                        tr_tMM[k] = tr_half
                    p.wait(S, "tMM", tr_half)
                    ins = S.copy(xnT[:, half * 8:(half + 1) * 8,
                                     k * 128:(k + 1) * 128],
                                 ptr[half][:].rearrange("p (a b) -> p a b", a=8))
                    xnt_half_sE[k][half] = p.inc(ins, "sE")
                    xnt_sE[k] = xnt_half_sE[k][half]
        all_xnT_sE = p.c("sE")

        p.wait(T, "sE", all_xnT_sE)
        p.wait(T, "vE", ones_vE)
        for c in range(2):
            for h in range(HT):
                ins = T.matmul(psS[:, c * 512:(c + 1) * 512], ones_col[:],
                               xnT[:, h, c * 512:(c + 1) * 512],
                               start=(h == 0), stop=(h == HT - 1))
            p.inc(ins, "tMM")
        srow_tMM = p.c("tMM")
        p.wait(S, "tMM", srow_tMM)
        ins = S.activation(scorr[0:1, :], psS[:], AF.Copy, scale=-128.0)
        hi_sE = p.inc(ins, "sE")
        p.wait(V, "sE", hi_sE)
        V.tensor_scalar_mul(row_tmp[:], psS[:], -128.0)
        V.tensor_copy(hi_f[:], scorr[0:1, :])
        vdummy()
        vdummy()
        ins = V.tensor_sub(lo_tmp[:], row_tmp[:], hi_f[:])
        lo_vE = p.inc(ins, "vE")
        p.wait(SY, "vE", lo_vE)
        scorr_qX = p.dma(SY, "qCorr", scorr[1:2, :], lo_tmp[:])
        scorr_sE = hi_sE

    with nc.psum_tensor("pg0", [128, 512]) as pg0, \
         nc.psum_tensor("pg1", [128, 512]) as pg1, \
         nc.psum_tensor("pg2", [128, 512]) as pg2, \
         nc.psum_tensor("pg3", [128, 512]) as pg3, \
         nc.psum_tensor("pg4", [128, 512]) as pg4, \
         nc.psum_tensor("pg5", [128, 512]) as pg5, \
         nc.psum_tensor("psSh", [1, 1024], F32) as psSh:
        pg = [pg0, pg1, pg2, pg3, pg4, pg5]
        wdma_qW = [0] * IM
        chain_done_tMM = [0] * IM
        evac_sE = {}
        evac_vE = {}
        stt_vE = [0] * IM
        silu_sE = [0] * IM

        def emit_wdma(m):
            if m >= 2:
                p.wait(G, "tMM", chain_done_tMM[m - 2])
            gq, uq = f"qWG{m % 2}", f"qWU{m % 2}"
            p.dma(G, gq, wgb[m % 2][:], wg[m])
            p.dma(G, uq, wub[m % 2][:], wu[m])
            wdma_qW[m] = (gq, p.c(gq), uq, p.c(uq))

        emit_wdma(0)
        emit_wdma(1)

        first_mm = True
        for m in range(IM):
            par = m % 2
            ch0 = 4 * m
            if first_mm:
                p.wait(T, "sE", scorr_sE)
                p.wait(T, "qCorr", scorr_qX)
                p.wait(T, "vE", ones_vE)
                first_mm = False
            gq, gv, uq, uv = wdma_qW[m]
            p.wait(T, gq, gv)
            p.wait(T, uq, uv)
            for j in range(4):
                chp = ch0 + j - 6
                if chp >= 0:
                    if chp in evac_sE:
                        p.wait(T, "sE", evac_sE.pop(chp))
                    else:
                        p.wait(T, "vE", evac_vE.pop(chp))
            bg = [pg[(ch0 + 0) % 6], pg[(ch0 + 1) % 6]]
            bu = [pg[(ch0 + 2) % 6], pg[(ch0 + 3) % 6]]
            for h in range(HT):
                for c in range(2):
                    T.matmul(bg[c][:], wgb[par][:, h * 128:(h + 1) * 128],
                             xnT[:, h, c * 512:(c + 1) * 512],
                             start=(h == 0), stop=False)
            g_tMM = [0, 0]
            for c in range(2):
                ins = T.matmul(bg[c][:], ones2[:], scorr[:, c * 512:(c + 1) * 512],
                               start=False, stop=True)
                g_tMM[c] = p.inc(ins, "tMM")
            for h in range(HT):
                for c in range(2):
                    T.matmul(bu[c][:], wub[par][:, h * 128:(h + 1) * 128],
                             xnT[:, h, c * 512:(c + 1) * 512],
                             start=(h == 0), stop=False)
            u_tMM = [0, 0]
            for c in range(2):
                ins = T.matmul(bu[c][:], ones2[:], scorr[:, c * 512:(c + 1) * 512],
                               start=False, stop=True)
                u_tMM[c] = p.inc(ins, "tMM")
            chain_done_tMM[m] = p.c("tMM")
            if m + 2 < IM:
                emit_wdma(m + 2)

            if m == 0:
                p.wait(S, "qSc", scales_qX)
                p.wait(V, "qSc", scales_qX)
            if m >= 2:
                p.wait(S, "vE", stt_vE[m - 2])
            sg_col = sgs[:, m:m + 1]
            s_cnt = [0, 0]
            for c in range(2):
                p.wait(S, "tMM", g_tMM[c])
                ins = S.activation(t1[par][:, c * 512:(c + 1) * 512], bg[c][:],
                                   AF.Silu, scale=sg_col)
                s_cnt[c] = p.inc(ins, "sE")
            evac_sE[ch0 + 0] = s_cnt[0]
            evac_sE[ch0 + 1] = s_cnt[1]
            silu_sE[m] = s_cnt[1]
            su_col = sus[:, m:m + 1]
            v_cnt = [0, 0]
            for c in range(2):
                p.wait(V, "tMM", u_tMM[c])
                p.wait(V, "sE", s_cnt[c])
                ins = V.scalar_tensor_tensor(hT[:, m, c * 512:(c + 1) * 512],
                                             bu[c][:], su_col,
                                             t1[par][:, c * 512:(c + 1) * 512],
                                             ALU.mult, ALU.mult)
                v_cnt[c] = p.inc(ins, "vE")
            evac_vE[ch0 + 2] = v_cnt[0]
            evac_vE[ch0 + 3] = v_cnt[1]
            stt_vE[m] = v_cnt[1]

            if m >= 2:
                ms = m - 2
                p.wait(T, "vE", stt_vE[ms])
                for c in range(2):
                    ins = T.matmul(psSh[:, c * 512:(c + 1) * 512], ones_col[:],
                                   hT[:, ms, c * 512:(c + 1) * 512],
                                   start=(ms == 0), stop=False)
        for ms in (IM - 2, IM - 1):
            p.wait(T, "vE", stt_vE[ms])
            for c in range(2):
                ins = T.matmul(psSh[:, c * 512:(c + 1) * 512], ones_col[:],
                               hT[:, ms, c * 512:(c + 1) * 512],
                               start=False, stop=(ms == IM - 1))
        sh_tMM = p.inc(ins, "tMM")
        p.wait(S, "tMM", sh_tMM)
        ins = S.activation(shcorr[0:1, :], psSh[:], AF.Copy, scale=-128.0)
        shhi_sE = p.inc(ins, "sE")
        p.wait(V, "sE", shhi_sE)
        p.wait(V, "vE", lo_vE)
        V.tensor_scalar_mul(row_tmp[:], psSh[:], -128.0)
        V.tensor_copy(hi_f[:], shcorr[0:1, :])
        vdummy()
        vdummy()
        ins = V.tensor_sub(lo_tmp[:], row_tmp[:], hi_f[:])
        shlo_vE = p.inc(ins, "vE")
        p.wait(SY, "vE", shlo_vE)
        shcorr_qX = p.dma(SY, "qCorr", shcorr[1:2, :], lo_tmp[:])
        all_stt_vE = p.c("vE")
        all_silu_sE = max(silu_sE)

    with nc.psum_tensor("pd0", [128, 512]) as pd0, \
         nc.psum_tensor("pd1", [128, 512]) as pd1, \
         nc.psum_tensor("pd2", [128, 512]) as pd2, \
         nc.psum_tensor("pd3", [128, 512]) as pd3:
        pd = [pd0, pd1, pd2, pd3]
        sub_qW = {}
        sub_tMM = {}
        gm_sE = {}
        kseq = 0
        def emit_wd_dma(m, q):
            k = 4 * m + q
            if k >= WDR_SLOTS:
                p.wait(G, "tMM", sub_tMM[k - WDR_SLOTS])
            qd = f"qWD{k % WDR_SLOTS}"
            v = p.dma(G, qd, wdr[k % WDR_SLOTS][:],
                      wd[m, :, q * 2048:(q + 1) * 2048])
            sub_qW[(m, q)] = (qd, v)

        p.wait(G, "tMM", chain_done_tMM[IM - 1])
        p.wait(G, "vE", all_stt_vE)
        p.wait(G, "sE", all_silu_sE)
        for k0 in range(WDR_SLOTS):
            emit_wd_dma(k0 // 4, k0 % 4)

        first_mm = True
        for m in range(HM):
            ch0 = 2 * m
            if first_mm:
                p.wait(T, "vE", all_stt_vE)
                p.wait(T, "sE", all_silu_sE)
                p.wait(T, "tMM", sh_tMM)
                p.wait(T, "sE", shhi_sE)
                p.wait(T, "qCorr", shcorr_qX)
                first_mm = False
            for j in range(2):
                chp = ch0 + j - 4
                if chp >= 0:
                    p.wait(T, "sE", gm_sE.pop(chp))
            b0, b1 = pd[(ch0 + 0) % 4], pd[(ch0 + 1) % 4]
            for u in range(IM):
                q = u // 16
                if u % 16 == 0:
                    qd, vd = sub_qW[(m, q)]
                    p.wait(T, qd, vd)
                slot = (4 * m + q) % WDR_SLOTS
                lh = wdr[slot][:, (u % 16) * 128:(u % 16 + 1) * 128]
                T.matmul(b0[:], lh, hT[:, u, 0:512], start=(u == 0), stop=False)
                ins = T.matmul(b1[:], lh, hT[:, u, 512:1024], start=(u == 0), stop=False)
                if u % 16 == 15:
                    k = 4 * m + q
                    sub_tMM[k] = p.inc(ins, "tMM")
                    nk = k + WDR_SLOTS
                    if nk < 4 * HM:
                        emit_wd_dma(nk // 4, nk % 4)
            d_tMM = [0, 0]
            for c, b in ((0, b0), (1, b1)):
                ins = T.matmul(b[:], ones2[:], shcorr[:, c * 512:(c + 1) * 512],
                               start=False, stop=True)
                d_tMM[c] = p.inc(ins, "tMM")
            sd_col = sds[:, m:m + 1]
            for c, b in ((0, b0), (1, b1)):
                if m == 0 and c == 0:
                    p.wait(S, "qSc", scales_qX)
                p.wait(S, "tMM", d_tMM[c])
                ins = S.activation(gmixT[:, m, c * 512:(c + 1) * 512], b[:],
                                   AF.Silu, scale=sd_col)
                gm_sE[ch0 + c] = p.inc(ins, "sE")
        gmix_done_sE = p.c("sE")
        endP4_tMM = p.c("tMM")

    with nc.psum_tensor("pm0", [64, 512]) as pm0, \
         nc.psum_tensor("pm1", [64, 512]) as pm1, \
         nc.psum_tensor("pm2", [64, 512]) as pm2, \
         nc.psum_tensor("pm3", [64, 512]) as pm3, \
         nc.psum_tensor("pm4", [64, 512]) as pm4, \
         nc.psum_tensor("pm5", [64, 512]) as pm5, \
         nc.psum_tensor("pm6", [64, 512]) as pm6, \
         nc.psum_tensor("pm7", [64, 512]) as pm7:
        pm = [pm0, pm1, pm2, pm3, pm4, pm5, pm6, pm7]
        engines = [nc.tensor, nc.vector, nc.scalar, nc.gpsimd, nc.sync]
        idx = [e.partition_id() for e in engines]
        for core in nc.Switch(engines=engines, index=idx, n=NCORE):
            groups = plans[core]
            for nm in (["tMMm", "vEm", "sEm", "gEm"]
                       + [f"qXCs{sl}" for sl in range(6)]
                       + [f"qWMs{sl}" for sl in range(6)]):
                p.cnt[nm] = 0

            p.wait(V, "tMM", endP4_tMM)
            ins = V.memset(s_acc[:], 0.0)
            memset_vEm = p.inc(ins, "vEm")
            p.wait(S, "sE", p.c("sE"))
            p.wait(S, "vE", p.c("vE"))
            p.wait(G, "tMM", endP4_tMM)
            for _ps in range(4):
                ins = G.memset(pad[_ps][:], 0.0)
            padz_gEm = p.inc(ins, "gEm")

            insts = []
            for gidx, (r_loc, n, bs) in enumerate(groups):
                for ci, b in enumerate(bs):
                    insts.append((b, n, gidx, r_loc, ci == 0, ci == len(bs) - 1))
            K = len(insts)
            wq_l = [None] * K
            padf_vEm = [0] * K
            padcp = [0] * K
            sq2m = [0] * K
            rs_vopm = [0] * K
            applym = [0] * K
            mm_done = [0] * K
            grp_evac = {}
            first_T = True
            for t in range(K + 4):
                if t < K:
                    b, n, gidx, r_loc, gfirst, glast = insts[t]
                    sl = t % 6
                    colv = int(plans["cols"][b])
                    qc = f"qXCs{sl}"
                    if t == 0:
                        p.wait(SY, "tMM", endP4_tMM)
                    if t >= 6:
                        p.wait(SY, "vEm", padf_vEm[t - 6])
                        p.wait(SY, "sEm", sq2m[t - 6])
                    src = xfull[n, 64 * colv:64 * colv + 64, :]
                    p.dma(SY, qc, xcb[sl][0:64, :], src)
                    ld_q = p.dma(SY, qc, xcb[sl][64:128, :], src)
                    qm = f"qWMs{sl}"
                    if t == 0:
                        p.wait(G, "tMM", endP4_tMM)
                    if t >= 6:
                        p.wait(G, "tMMm", mm_done[t - 6])
                    wq_l[t] = (qm, p.dma(G, qm, wmb[sl][:], wmix[b]))
                    cs = t % 6
                    p.wait(V, qc, ld_q)
                    vop(V.reduce_sum(vc(cs, V_SUM), xcb[sl][:], axis=AX.X))
                    p.wait(S, qc, ld_q)
                    if t >= 1:
                        p.wait(S, "sEm", sq2m[t - 1])
                    ins = S.activation(junk[:], xcb[sl][:], AF.Square,
                                       accum_out=scc(cs, S_SQ))
                    sq2m[t] = p.inc(ins, "sEm")
                if t >= 2 and t - 2 < K:
                    j = t - 2
                    cs = j % 6
                    vop(V.tensor_scalar_mul(vc(cs, V_NM), vc(cs, V_SUM), -1.0 / D))
                    p.wait(V, "sEm", sq2m[j])
                    vop(V.tensor_scalar(vc(cs, V_V1), scc(cs, S_SQ), 1.0 / D, EPS,
                                        ALU.mult, ALU.add))
                    vop(V.tensor_mul(vc(cs, V_VT), vc(cs, V_NM), vc(cs, V_NM)))
                    vdummy()
                    vdummy()
                    ins = vop(V.tensor_sub(vc(cs, V_VE), vc(cs, V_V1), vc(cs, V_VT)))
                    ve_v = p.inc(ins, "vEm")
                    p.wait(S, "vEm", ve_v)
                    ins = S.activation(scc(cs, S_SD), vc(cs, V_VE), AF.Sqrt)
                    sd_s = p.inc(ins, "sEm")
                    p.wait(V, "sEm", sd_s)
                    vop(V.reciprocal(vc(cs, V_RS), scc(cs, S_SD)))
                    rs_vopm[j] = nvops[0]
                if t >= 4:
                    k = t - 4
                    b, n, gidx, r_loc, gfirst, glast = insts[k]
                    sl = k % 6
                    ps = k % 4
                    cs = k % 6
                    pf = k % 2
                    vpad(rs_vopm[k] + 8)
                    if k >= 2:
                        p.wait(V, "sEm", padcp[k - 2])
                    ins = vop(V.tensor_scalar(padf[pf][:], xcb[sl][:],
                                              vc(cs, V_NM), vc(cs, V_RS),
                                              ALU.add, ALU.mult))
                    padf_vEm[k] = p.inc(ins, "vEm")
                    p.wait(S, "vEm", padf_vEm[k])
                    if k < 4:
                        p.wait(S, "gEm", padz_gEm)
                    else:
                        p.wait(S, "tMMm", mm_done[k - 4])
                    S.copy(pad[ps][0:64, 15:2063], padf[pf][0:64, :])
                    ins = S.copy(pad[ps][64:128, 14:2062], padf[pf][64:128, :])
                    padcp[k] = p.inc(ins, "sEm")
                    applym[k] = padcp[k]
                    banks = pm[4 * (gidx % 2):4 * (gidx % 2) + 4]
                    if first_T:
                        p.wait(T, "tMM", endP4_tMM)
                        p.wait(T, "sE", gmix_done_sE)
                        first_T = False
                    if gfirst and gidx >= 2:
                        p.wait(T, "sEm", grp_evac[gidx - 2])
                    p.wait(T, wq_l[k][0], wq_l[k][1])
                    p.wait(T, "sEm", applym[k])
                    ins = None
                    for jj in range(8):
                        lh = wmb[sl][:, jj * 64:(jj + 1) * 64]
                        for c in range(4):
                            ins = T.matmul(
                                banks[c][:], lh,
                                pad[ps][:, c * 512 + 2 * jj:c * 512 + 2 * jj + 512],
                                start=(gfirst and jj == 0),
                                stop=(glast and jj == 7))
                    mm_done[k] = p.inc(ins, "tMMm")
                    if glast:
                        tile_idx = n * (NTILE // 2) + r_loc // 2
                        p0 = (r_loc % 2) * 64
                        p.wait(S, "tMMm", mm_done[k])
                        p.wait(S, "vEm", memset_vEm)
                        ins = None
                        for c in range(4):
                            ins = S.copy(s_acc[p0:p0 + 64, tile_idx,
                                               c * 512:(c + 1) * 512],
                                         banks[c][0:64, :])
                        grp_evac[gidx] = p.inc(ins, "sEm")

            V.tensor_copy(col(C_VFEN), col(C_SUM)).then_inc(p.sems["mph"], 1)
            S.copy(col(C_SFEN), col(C_SQ)).then_inc(p.sems["mph"], 1)
            T.nop().then_inc(p.sems["mph"], 1)
            G.nop().then_inc(p.sems["mph"], 1)
            SY.nop().then_inc(p.sems["mph"], 1)
        p.cnt["mph"] = 5

    with nc.psum_tensor("pt60", [128, 2048], BF16) as pt60, \
         nc.psum_tensor("pt61", [128, 2048], BF16) as pt61:
        pt6 = [pt60, pt61]
        comb_vE = [0] * NTILE
        store_qS = [0] * NTILE
        tr6_tMM = [0] * NTILE
        for i in range(NTILE):
            par = i % 2
            if i == 0:
                p.wait(SY, "mph", 5)
                p.wait(T, "mph", 5)
                p.wait(V, "mph", 5)
                p.wait(S, "mph", 5)
            qn = "qXT0" if par == 0 else "qXT1"
            if i >= 2:
                p.wait(SY, "vE", comb_vE[i - 2])
            x_q = (qn, p.dma(SY, qn, xt6[par][:], xown_tile(i)))
            if i >= 2:
                p.wait(T, "vE", comb_vE[i - 2])
            ins = None
            for m in range(HM):
                ins = T.transpose(pt6[par][:, m * 128:(m + 1) * 128],
                                  gmixT[:, m, i * 128:(i + 1) * 128], ident[:])
            tr6_tMM[i] = p.inc(ins, "tMM")
            p.wait(V, x_q[0], x_q[1])
            p.wait(V, "tMM", tr6_tMM[i])
            if i >= 2:
                p.wait(V, store_qS[i - 2][0], store_qS[i - 2][1])
            V.tensor_tensor(ost[par][:], s_acc[:, i, :], pt6[par][:], ALU.mult)
            vdummy()
            vdummy()
            ins = V.tensor_add(ost[par][:], ost[par][:], xt6[par][:])
            comb_vE[i] = p.inc(ins, "vE")
            p.wait(S, "vE", comb_vE[i])
            n, s0 = i // (SLAB // 128), (i * 128) % SLAB
            qs = "qS0" if par == 0 else "qS1"
            store_qS[i] = (qs, p.dma(S, qs, y[n, s0:s0 + 128, :], ost[par][:]))
        p.wait(S, "qS0", p.c("qS0"))
        p.wait(S, "qS1", p.c("qS1"))
        p.wait(SY, "qS0", p.c("qS0"))
        p.wait(SY, "qS1", p.c("qS1"))

    return nc


def _plan(rows):
    plans = {}
    for c in range(NCORE):
        groups = []
        for r_loc in range(8):
            bs = [b for b in range(128) if int(rows[b]) == 8 * c + r_loc]
            if bs:
                for n in range(BATCH):
                    groups.append((r_loc, n, bs))
        plans[c] = groups
    return plans


def kernel(L0_attn_x, L0_attn_block_values, L0_attn_cols, L0_attn_rows,
           L0_ffn_gate_w, L0_ffn_gate_scale, L0_ffn_up_w, L0_ffn_up_scale,
           L0_ffn_down_w, L0_ffn_down_scale):
    x = np.ascontiguousarray(np.asarray(L0_attn_x, dtype=np.float32))
    blockv = np.asarray(L0_attn_block_values, dtype=np.float32)
    cols = np.asarray(L0_attn_cols, dtype=np.int32)
    rows = np.asarray(L0_attn_rows, dtype=np.int32)
    gw = np.asarray(L0_ffn_gate_w, dtype=np.int32)
    uw = np.asarray(L0_ffn_up_w, dtype=np.int32)
    dw = np.asarray(L0_ffn_down_w, dtype=np.int32)

    gp = np.ascontiguousarray(
        gw.reshape(IM, 128, HT, 128).transpose(0, 3, 2, 1)).reshape(IM, 128, HT * 128)
    up = np.ascontiguousarray(
        uw.reshape(IM, 128, HT, 128).transpose(0, 3, 2, 1)).reshape(IM, 128, HT * 128)
    dp = np.ascontiguousarray(
        dw.reshape(HM, 128, IM, 128).transpose(0, 3, 2, 1)).reshape(HM, 128, IM * 128)
    sgp = np.ascontiguousarray(
        np.asarray(L0_ffn_gate_scale, dtype=np.float32).reshape(IM, 128).T)
    sup = np.ascontiguousarray(
        np.asarray(L0_ffn_up_scale, dtype=np.float32).reshape(IM, 128).T)
    sdp = np.ascontiguousarray(
        np.asarray(L0_ffn_down_scale, dtype=np.float32).reshape(HM, 128).T)
    wm = np.ascontiguousarray(
        blockv.reshape(128, 64, 64, 8, 2).transpose(0, 4, 2, 3, 1)
    ).reshape(128, 128, 512)

    plans = _plan(rows)
    plans["cols"] = cols
    nc = build_program(plans)

    in_maps = []
    for c in range(NCORE):
        in_maps.append({
            "xown": np.ascontiguousarray(x[:, SLAB * c:SLAB * (c + 1), :]),
            "xfull": x,
            "wg": gp, "wu": up, "wd": dp,
            "sg_in": sgp, "su_in": sup, "sd_in": sdp,
            "wmix": wm,
        })
    res = run_bass_kernel_spmd(nc, in_maps, core_ids=list(range(NCORE)))
    out = np.empty((BATCH, SEQ, D), dtype=np.float32)
    for c in range(NCORE):
        out[:, SLAB * c:SLAB * (c + 1), :] = res.results[c]["y"]
    return out
